# version 11
# baseline (speedup 1.0000x reference)
import numpy as np
import ml_dtypes
import concourse.bass as bass
import concourse.mybir as mybir
from concourse.bass_utils import run_bass_kernel_spmd
from contextlib import ExitStack

F32 = mybir.dt.float32
BF16 = mybir.dt.bfloat16
AF = mybir.ActivationFunctionType
ALU = mybir.AluOpType
AX = mybir.AxisListType

W = 8
B = 2
T = 8192
D = 1024
N = 16
R = 64
DFF = 2048
NE = 8
TC = 2048
L = 512
NT = TC // L
G = D // 128
FC = DFF // 128
EPS = 1e-5

MARKERS_OFF = True
PREFETCH = True
ENGS = "qgtvs"
DSEMS = ["sm", "gw", "asm", "qld0", "qld1", "qst0", "qst1",
         "xt0", "xt1", "xh", "hs", "zs", "cs", "ys", "br0", "br1",
         "st", "hf", "e1", "e2", "cr0", "cr1", "yr0", "yr1", "fin", "os", "dbg"]


class Emitter:
    def __init__(self):
        self.ops = {e: [] for e in ENGS}
        self.cnt = {e: 0 for e in ENGS}
        self.dcnt = {d: 0 for d in DSEMS}

    def op(self, e, f):
        self.ops[e].append(("op", f))

    def inc(self, e, f):
        self.cnt[e] += 1
        self.ops[e].append(("inc1", f))
        return (e, self.cnt[e])

    def dma(self, e, dsem, f):
        self.dcnt[dsem] += 16
        self.ops[e].append(("dma", (dsem, f)))
        return (dsem, self.dcnt[dsem])

    def dfull(self, dsem):
        return (dsem, self.dcnt[dsem])

    def wait(self, e, tok):
        if tok is None:
            return
        name, n = tok
        if name == e:
            return
        self.ops[e].append(("wait", (name, n)))

    def selfsync(self, e):
        self.cnt[e] += 1
        self.ops[e].append(("nop1", None))
        self.ops[e].append(("wait", (e + "!", self.cnt[e])))

    def marker(self, e):
        self.cnt[e] += 1
        self.ops[e].append(("nop1", None))
        return (e, self.cnt[e])

    def incm(self, e, f=None, site=""):
        if MARKERS_OFF is True or (isinstance(MARKERS_OFF, (set, frozenset)) and site in MARKERS_OFF):
            return self.inc(e, f)
        self.inc(e, f)
        return self.marker(e)

    def barrier(self, light=()):
        toks = {e: self.marker(e) for e in ENGS}
        for e in ENGS:
            for oe in ENGS:
                if oe != e and oe not in light:
                    self.wait(e, toks[oe])

    def run(self, block, sems):
        for key in ENGS:
            lst = self.ops[key]
            if not lst:
                continue
            body = {"q": block.sync, "g": block.gpsimd, "t": block.tensor,
                    "v": block.vector, "s": block.scalar}[key]

            def make_body(lst=lst, key=key):
                def _body(eng):
                    for kind, payload in lst:
                        if kind == "op":
                            payload(eng)
                        elif kind == "inc1":
                            payload(eng).then_inc(sems[key], 1)
                        elif kind == "dma":
                            dsem, f = payload
                            f(eng).then_inc(sems[dsem], 16)
                        elif kind == "wait":
                            name, n = payload
                            eng.wait_ge(sems[name.rstrip("!")], n)
                        elif kind == "nop1":
                            eng.nop().then_inc(sems[key], 1)
                return _body
            body(make_body())
        self.ops = {e: [] for e in ENGS}


def build_nc(dbg=False, stage="full"):
    nc = bass.Bass("TRN2", target_bir_lowering=False, debug=False,
                   detect_race_conditions=False)

    def inp(name, shape, dt=F32):
        return nc.dram_tensor(name, list(shape), dt, kind="ExternalInput").ap()

    x_h = inp("x_slice", (TC + 2, D))
    winT_h = inp("w_inT", (D, 2 * D))
    woutT_h = inp("w_outT", (D, D))
    wdtT_h = inp("w_dtT", (R, D))
    wcT_h = inp("w_cT", (N, D))
    wxa_h = inp("w_xTA", (D, R + N))
    wxb_h = inp("w_xTB", (D, N))
    rwT_h = inp("r_wT", (D, NE))
    alog_h = inp("alog_pack", (128, G * N))
    conv_h = inp("conv_pack", (128, G * 3))
    vec_h = inp("vec_pack", (128, 32))
    identb_h = inp("ident_bf", (128, 128), BF16)
    identf_h = inp("ident_f", (128, 128), F32)
    ucoef_h = inp("ucoef", (128, 8))
    qw1_h = inp("qw1", (DFF, D))
    qw2_h = inp("qw2", (D, DFF))
    qs1_h = inp("qs1", (DFF // W, D))
    qs2_h = inp("qs2", (D // W, DFF))

    out_h = nc.dram_tensor("out", [TC, D], F32, kind="ExternalOutput").ap()

    def intern(name, shape, dt=F32, dbg_vis=False):
        kind = "ExternalOutput" if (dbg and dbg_vis) else "Internal"
        return nc.dram_tensor(name, list(shape), dt, kind=kind).ap()

    h_sp = intern("h_sp", (NT * G * 128, L), BF16, dbg_vis=True)
    xn_sp = intern("xn_sp", (G * 128, L), BF16, dbg_vis=(stage == "dbgz"))
    gate_sp = intern("gate_sp", (NT * 128, 4 * NE), F32, dbg_vis=True)
    e1_sp = intern("e1_sp", (G * 128, DFF), BF16, dbg_vis=True)
    e2_sp = intern("e2_sp", (FC * 128, D), BF16, dbg_vis=True)
    s1_sp = intern("s1_sp", (G * 128, DFF), BF16, dbg_vis=True)
    wqb_sp = intern("wqb_sp", (2 * 128, DFF), BF16, dbg_vis=True)
    qst_sp = intern("qst_sp", (2 * 128, DFF), BF16, dbg_vis=True)
    s2_sp = intern("s2_sp", (FC * 128, D), BF16, dbg_vis=True)
    acc_sp = intern("acc_sp", (NT * 4 * 128, D), F32, dbg_vis=True)
    cdt_sp = intern("cdt_sp", (NT * G * 128, L), F32, dbg_vis=True)
    siluz_sp = intern("siluz_sp", (NT * 4 * 128, D), F32, dbg_vis=True)
    yp_sp = intern("yp_sp", (NT * G * 128, L), F32, dbg_vis=True)
    qT_ew1 = intern("qT_ew1", (D, DFF), BF16)
    qT_ew2 = intern("qT_ew2", (DFF, D), BF16)
    qT_s1 = intern("qT_s1", (D, DFF // W), BF16)
    qT_s2 = intern("qT_s2", (DFF, D // W), BF16)
    ag_ew1 = intern("ag_ew1", (W * D, DFF), BF16)
    ag_ew2 = intern("ag_ew2", (W * DFF, D), BF16)
    ag_s1 = intern("ag_s1", (W * D, DFF // W), BF16)
    ag_s2 = intern("ag_s2", (W * DFF, D // W), BF16)
    sw1T_lin = intern("sw1T_lin", (D, DFF), BF16)
    sw2T_lin = intern("sw2T_lin", (DFF, D), BF16)
    st_in = intern("st_in", (128, 2 * G * N))
    st_out = intern("st_out", (W * 128, 2 * G * N))

    em = Emitter()
    q, g, t, v, s = "q", "g", "t", "v", "s"

    with ExitStack() as outer:
        e_ = outer.enter_context
        nega = e_(nc.sbuf_tensor("nega", [128, G * N], F32))
        convp = e_(nc.sbuf_tensor("convp", [128, G * 3], F32))
        vecp = e_(nc.sbuf_tensor("vecp", [128, 32], F32))
        identb = e_(nc.sbuf_tensor("identb", [128, 128], BF16))
        identf = e_(nc.sbuf_tensor("identf", [128, 128], F32))
        ucoef = e_(nc.sbuf_tensor("s_ucoef", [128, 8], F32))
        ones_col = e_(nc.sbuf_tensor("ones_col", [1, 128], F32))
        constv = e_(nc.sbuf_tensor("constv", [128, 4], F32))
        nc.const_aps.register(F32, EPS, constv[:, 0:1])
        nc.const_aps.register(F32, -1.0, constv[:, 1:2])
        nc.const_aps.register(F32, 1.0, constv[:, 2:3])
        nc.const_aps.register(F32, 0.0, constv[:, 3:4])
        ones_row = e_(nc.sbuf_tensor("ones_row", [128, L], F32))
        scarry = [e_(nc.sbuf_tensor(f"scarry{j}", [128, 1], F32)) for j in range(G * N)]
        cdtcarry = [e_(nc.sbuf_tensor(f"cdtcarry{j}", [128, 1], F32)) for j in range(G)]
        scarry_cat = e_(nc.sbuf_tensor("scarry_cat", [128, G * N], F32))
        cp_bf = [e_(nc.sbuf_tensor(f"cpbf{i}", [N, L], BF16)) for i in range(NT)]
        s_in = e_(nc.sbuf_tensor("s_in", [128, G * N], F32))
        wc = e_(nc.sbuf_tensor("wc", [N, D], BF16))
        ps = [e_(nc.psum_tensor(f"ps{i}", [128, L], F32)) for i in range(6)]
        ps_t2 = [e_(nc.psum_tensor("ps_t0", [128, 128], BF16))]
        ps_tb2 = e_(nc.psum_tensor("ps_tb2", [128, 128], BF16))

        sems = {key: e_(nc.semaphore("sem_" + key)) for key in ENGS}
        for d in DSEMS:
            sems[d] = e_(nc.semaphore("dsm_" + d))
        csem = e_(nc.semaphore("sem_c"))
        ccnt = [0]
        block = e_(nc.Block())

        with ExitStack() as prep_scope:
            pe_ = prep_scope.enter_context
            qbuf = [pe_(nc.sbuf_tensor(f"qbuf{i}", [128, DFF], F32)) for i in range(2)]
            tq = pe_(nc.sbuf_tensor("tq", [128, DFF], F32))
            mng2 = [pe_(nc.sbuf_tensor(f"mng{j}", [128, DFF], F32)) for j in range(2)]
            wqb = pe_(nc.sbuf_tensor("wqb", [128, DFF], BF16))
            qstage = [pe_(nc.sbuf_tensor(f"qstage{i}", [128, DFF], BF16)) for i in range(2)]
            qcol = pe_(nc.sbuf_tensor("qcol", [128, 8], F32))
            atmp = pe_(nc.sbuf_tensor("atmp", [128, G * N], F32))

            em.dma(q, "sm", lambda e: e.dma_start(out=atmp[:, :], in_=alog_h))
            em.dma(q, "sm", lambda e: e.dma_start(out=convp[:, :], in_=conv_h))
            em.dma(q, "sm", lambda e: e.dma_start(out=vecp[:, :], in_=vec_h))
            em.dma(q, "sm", lambda e: e.dma_start(out=identb[:, :], in_=identb_h))
            em.dma(q, "sm", lambda e: e.dma_start(out=identf[:, :], in_=identf_h))
            em.dma(q, "sm", lambda e: e.dma_start(out=ucoef[:, :], in_=ucoef_h))
            em.dma(g, "gw", lambda e: e.dma_start(out=wc[:, :], in_=wcT_h))
            tok_sm = em.dfull("sm")

            em.op(v, lambda e: e.memset(ones_col[:, :], 1.0))
            em.op(v, lambda e: e.memset(constv[:, 0:1], EPS))
            em.op(v, lambda e: e.memset(constv[:, 1:2], -1.0))
            em.op(v, lambda e: e.memset(constv[:, 2:3], 1.0))
            tok_const = em.inc(v, lambda e: e.memset(constv[:, 3:4], 0.0))
            em.op(v, lambda e: e.memset(ones_row[:, :], 1.0))
            for j in range(G * N):
                em.op(v, lambda e, j=j: e.memset(scarry[j][:, :], 0.0))
            for j in range(G):
                em.op(v, lambda e, j=j: e.memset(cdtcarry[j][:, :], 0.0))
            em.inc(v, lambda e: e.memset(scarry_cat[:, :], 0.0))

            em.wait(s, tok_const)
            em.wait(g, tok_const)
            em.wait(t, tok_const)
            em.wait(s, tok_sm)
            tk_aexp = em.inc(s, lambda e: e.activation(atmp[:, :], atmp[:, :], AF.Exp))
            em.wait(v, tk_aexp)
            em.inc(v, lambda e: e.tensor_scalar(nega[:, :], atmp[:, :], -1.0, None, ALU.mult))

            qjobs = [
                (qw1_h, DFF // 128, D, qT_ew1),
                (qw2_h, D // 128, DFF, qT_ew2),
                (qs1_h, (DFF // W) // 128, D, qT_s1),
                (qs2_h, (D // W) // 128, DFF, qT_s2),
            ]
            if stage == "prep_a":
                qjobs = []
            free_qbuf = [None, None]
            free_qstage = [None, None]
            mng_free = [None, None]
            qcol_free = [None, None]
            tok_qsum_sl = [None, None]
            last_tr_evac = {}
            tok_wqb_read = [None]
            gather_pairs = [(qT_ew1, ag_ew1), (qT_ew2, ag_ew2), (qT_s1, ag_s1), (qT_s2, ag_s2)]
            pending_gathers = []

            def gather_dst(src):
                for a, b in gather_pairs:
                    if a is src:
                        return b
                raise KeyError
            chunks = []
            for (src, nrc, K, dstT) in qjobs:
                for rc in range(nrc):
                    chunks.append((src, rc, K, dstT, rc == nrc - 1))

            def q_stage1(c):
                src, rc, K, dstT, _ = chunks[c]
                sl = c % 2
                em.wait(q, free_qbuf[sl])
                tok_load = em.dma(q, "qld" + str(sl),
                                  lambda e, src=src, rc=rc, sl=sl, K=K:
                                  e.dma_start(out=qbuf[sl][:, 0:K],
                                              in_=src[rc * 128:(rc + 1) * 128, :]))
                em.wait(s, tok_load)
                em.wait(s, mng_free[sl])
                em.wait(s, qcol_free[sl])
                tok_qsum_sl[sl] = em.inc(s, lambda e, sl=sl, K=K: e.activation(
                    mng2[sl][:, 0:K], qbuf[sl][:, 0:K], AF.Abs,
                    accum_out=qcol[:, 4 * sl:4 * sl + 1]))

            def q_rest(c):
                src, rc, K, dstT, is_last = chunks[c]
                sl = c % 2
                qc = lambda j: qcol[:, 4 * sl + j:4 * sl + j + 1]
                em.wait(v, tok_qsum_sl[sl])
                tok_qs1 = em.inc(v, lambda e, K=K, qc=qc: e.tensor_scalar(
                    qc(1), qc(0), 1.0 / K, 1e-5, ALU.mult, ALU.max))
                em.wait(s, tok_qs1)
                tok_b1 = em.inc(s, lambda e, qc=qc: e.copy(qc(3), qc(1)))
                em.wait(v, tok_b1)
                tok_rec = em.inc(v, lambda e, qc=qc: e.reciprocal(qc(2), qc(1)))
                em.wait(s, tok_rec)
                tok_b2 = em.inc(s, lambda e, qc=qc: e.copy(qc(3), qc(2)))
                em.wait(v, tok_b2)
                tok_tq = em.inc(v, lambda e, sl=sl, K=K, qc=qc: e.tensor_scalar(
                    tq[:, 0:K], qbuf[sl][:, 0:K], qc(2), None, ALU.mult))
                free_qbuf[sl] = tok_tq
                em.selfsync(v)
                em.op(v, lambda e, sl=sl, K=K: e.tensor_scalar(
                    mng2[sl][:, 0:K], tq[:, 0:K], -0.5, None, ALU.is_lt))
                em.selfsync(v)
                tok_stt = em.inc(v, lambda e, sl=sl, K=K: e.scalar_tensor_tensor(
                    tq[:, 0:K], tq[:, 0:K], 0.5, mng2[sl][:, 0:K],
                    ALU.is_gt, ALU.subtract))
                mng_free[sl] = tok_stt
                em.selfsync(v)
                em.wait(v, tok_wqb_read[0])
                tok_wqbw = em.inc(v, lambda e, K=K, qc=qc: e.tensor_scalar(
                    wqb[:, 0:K], tq[:, 0:K], qc(1), None, ALU.mult))
                qcol_free[sl] = tok_wqbw
                em.selfsync(v)
                em.op(v, lambda e, sl=sl: e.tensor_copy(mng2[sl][:, 0:256], wqb[:, 0:256]))
                em.selfsync(v)
                tok_wqb = em.inc(v, lambda e, sl=sl, K=K: e.tensor_copy(
                    mng2[sl][:, 256:512], wqb[:, K - 256:K]))
                ssl = sl
                em.wait(t, tok_wqb)
                em.wait(t, tok_sm)
                em.wait(v, free_qstage[ssl])
                tok_ev = None
                for j in range(K // 128):
                    em.wait(t, last_tr_evac.get(0))
                    em.inc(t, lambda e, j=j: e.transpose(
                        ps_t2[0][:, :],
                        wqb[:, j * 128:(j + 1) * 128], identb[:, :]))
                    tok_tr = em.marker(t)
                    em.wait(v, tok_tr)
                    tok_ev = em.inc(v, lambda e, j=j, ssl=ssl: e.tensor_copy(
                        qstage[ssl][:, j * 128:(j + 1) * 128],
                        ps_t2[0][:, :]))
                    last_tr_evac[0] = tok_ev
                tok_wqb_read[0] = ("t", em.cnt["t"])
                em.wait(q, tok_ev)
                if dbg and c in (0, 3):
                    di = 0 if c == 0 else 1
                    em.dma(q, "dbg", lambda e, di=di, K=K: e.dma_start(
                        out=wqb_sp[di * 128:(di + 1) * 128, 0:K], in_=wqb[:, 0:K]))
                    em.dma(q, "dbg", lambda e, di=di, ssl=ssl, K=K: e.dma_start(
                        out=qst_sp[di * 128:(di + 1) * 128, 0:K], in_=qstage[ssl][:, 0:K]))
                    em.wait(v, em.dfull("dbg"))
                    em.wait(t, em.dfull("dbg"))
                tok_qdma = em.dma(q, "qst" + str(ssl),
                                  lambda e, ssl=ssl, rc=rc, K=K, dstT=dstT: e.dma_start(
                    out=dstT[0:K, rc * 128:(rc + 1) * 128].rearrange("(j p) c -> p j c", p=128),
                    in_=qstage[ssl][:, 0:K].rearrange("p (j c) -> p j c", c=128)))
                free_qstage[ssl] = tok_qdma
                if is_last and stage not in ("prep_a", "prep_b"):
                    ccnt[0] += 1
                    pending_gathers.append((em.dfull("qst0"), em.dfull("qst1"),
                                            dstT, gather_dst(dstT)))

            for c in range(len(chunks) + 1):
                if c < len(chunks):
                    q_stage1(c)
                if c >= 1:
                    q_rest(c - 1)

            wag_cnt = ccnt[0]

            em.barrier(light=("g",))
            if stage.startswith("prep"):
                for d in DSEMS:
                    if em.dcnt[d]:
                        em.wait(q, em.dfull(d))
                em.marker(q)
            em.run(block, dict(sems, c=csem))
        if stage.startswith("prep"):
            return nc

        with ExitStack() as p1_scope:
            p1 = p1_scope.enter_context
            wi = [p1(nc.sbuf_tensor(f"wi{k}", [128, 2 * D], BF16)) for k in range(G)]
            wdt = p1(nc.sbuf_tensor("wdt", [R, D], BF16))
            wxa = [p1(nc.sbuf_tensor(f"wxa{k}", [128, R + N], BF16)) for k in range(G)]
            wxb = [p1(nc.sbuf_tensor(f"wxb{k}", [128, N], BF16)) for k in range(G)]
            xtok = [p1(nc.sbuf_tensor(f"xtok{i}", [128, D], F32)) for i in range(2)]
            xhalo = p1(nc.sbuf_tensor("xhalo", [2, D], F32))
            lncol = p1(nc.sbuf_tensor("lncol", [128, 16], F32))
            sqscr = p1(nc.sbuf_tensor("sqscr", [128, D], F32))
            xnt_bf = p1(nc.sbuf_tensor("xnt_bf", [128, D], BF16))
            xn = [p1(nc.sbuf_tensor(f"xn{k}", [128, L + 2], BF16)) for k in range(G)]
            hfm = [p1(nc.sbuf_tensor(f"hfm{k}", [128, L], BF16)) for k in range(G)]
            xcs = [p1(nc.sbuf_tensor(f"xcs{i}", [128, L + 2], F32)) for i in range(2)]
            convt = p1(nc.sbuf_tensor("convt", [128, L], F32))
            xsf = [p1(nc.sbuf_tensor(f"xsf{k}", [128, L], F32)) for k in range(G)]
            xsb = [p1(nc.sbuf_tensor(f"xsb{k}", [128, L], BF16)) for k in range(G)]
            dtp_bf = p1(nc.sbuf_tensor("dtp_bf", [R, L], BF16))
            bp_f = p1(nc.sbuf_tensor("bp_f", [N, L], F32))
            bp_row = [p1(nc.sbuf_tensor(f"bprow{i}", [1, L], F32)) for i in range(2)]
            bpb = p1(nc.sbuf_tensor("bpb", [128, L], F32))
            dt_h = [p1(nc.sbuf_tensor(f"dth{i}", [128, L], F32)) for i in range(4)]
            cdt_s = [p1(nc.sbuf_tensor(f"cdts{i}", [128, L], F32)) for i in range(2)]
            dtxs_h = [p1(nc.sbuf_tensor(f"dtxsh{i}", [128, L], F32)) for i in range(4)]
            ssum_h = [p1(nc.sbuf_tensor(f"ssumh{i}", [128, L], F32)) for i in range(4)]
            yps = [p1(nc.sbuf_tensor(f"yps{i}", [128, L], F32)) for i in range(2)]
            sa_s = [p1(nc.sbuf_tensor(f"sas{i}", [128, L], F32)) for i in range(2)]
            sbx_s = [p1(nc.sbuf_tensor(f"sbxs{i}", [128, L], F32)) for i in range(2)]
            ss_s = [p1(nc.sbuf_tensor(f"sss{i}", [128, L], F32)) for i in range(2)]
            szslot = [p1(nc.sbuf_tensor(f"szslot{i}", [128, D], F32)) for i in range(2)]
            sp1 = p1(nc.sbuf_tensor("sp1", [128, L], F32))
            sp2 = p1(nc.sbuf_tensor("sp2", [128, L], F32))
            sp3 = p1(nc.sbuf_tensor("sp3", [128, L], F32))
            rm_sb = p1(nc.sbuf_tensor("rm_sb", [1, 2 * L], F32))

            for k in range(G):
                em.dma(g, "gw", lambda e, k=k: e.dma_start(
                    out=wi[k][:, :], in_=winT_h[k * 128:(k + 1) * 128, :]))
                em.dma(g, "gw", lambda e, k=k: e.dma_start(
                    out=wxa[k][:, :], in_=wxa_h[k * 128:(k + 1) * 128, :]))
                em.dma(g, "gw", lambda e, k=k: e.dma_start(
                    out=wxb[k][:, :], in_=wxb_h[k * 128:(k + 1) * 128, :]))
            em.dma(g, "gw", lambda e: e.dma_start(out=wdt[:, :], in_=wdtT_h))
            for (tq0, tq1, gsrc, gdst) in pending_gathers:
                em.wait(g, tq0)
                em.wait(g, tq1)
                em.op(g, lambda e, gsrc=gsrc, gdst=gdst: e.collective_compute(
                    "AllGather", ALU.bypass, replica_groups=[list(range(W))],
                    ins=[gsrc.opt()], outs=[gdst.opt()]).then_inc(csem, 1))
            tok_wmix = em.dfull("gw")
            em.wait(t, tok_wmix)
            em.wait(v, tok_wmix)
            em.wait(s, tok_wmix)
            em.wait(q, tok_wmix)
            em.wait(g, tok_wmix)

            free_xtok = [None, None]
            free_xhalo = None
            cdt_carry_done = [None] * G
            s_carry_done = [None] * (G * N)
            free_xcs = [None, None]
            free_cdt = [None, None]
            free_yps = [None, None]
            free_sa = [None, None]
            free_sbx = [None, None]
            free_ss = [None, None]
            free_ss_scpy = [None, None]
            free_bprow = [None, None]
            free_ps = [None] * 6
            free_pst = [None, None]
            rm_free = [None]
            free_bpb = [None]
            tok_xn_lastuse = [None] * G
            psum_rot = [0]
            prev_tok_dt = [None]
            last_scan_tok = None

            def next_ps01():
                idx = psum_rot[0] % 4
                psum_rot[0] += 1
                return idx

            def emit_ln(src, npart, tok_trlast):
                cc = lambda j: lncol[0:npart, j:j + 1]
                em.wait(s, ("v", em.cnt["v"]))
                em.wait(v, tok_trlast)
                em.wait(v, em.dfull("br0"))
                if stage == "dbg4":
                    em.wait(v, em.dfull("zs"))
                em.op(v, lambda e: e.tensor_reduce(cc(0), src, AX.X, ALU.add))
                tok_sq = em.inc(s, lambda e: e.activation(
                    sqscr[0:npart, :], src, AF.Square, accum_out=cc(1)))
                em.wait(v, tok_sq)
                em.selfsync(v)
                em.op(v, lambda e: e.tensor_scalar(cc(2), cc(0), 1.0 / D, None, ALU.mult))
                em.selfsync(v)
                em.op(v, lambda e: e.scalar_tensor_tensor(
                    cc(7), cc(2), -float(D), cc(0), ALU.mult, ALU.add))
                em.op(v, lambda e: e.tensor_scalar(
                    cc(3), cc(2), cc(2), float(D), ALU.mult, ALU.mult))
                em.op(v, lambda e: e.scalar_tensor_tensor(
                    cc(14), cc(2), -2.0, cc(0), ALU.mult, ALU.mult))
                em.selfsync(v)
                em.op(v, lambda e: e.tensor_tensor(cc(15), cc(3), cc(14), ALU.add))
                em.selfsync(v)
                em.op(v, lambda e: e.tensor_tensor(cc(4), cc(15), cc(1), ALU.add))
                em.selfsync(v)
                tok_var = em.inc(v, lambda e: e.tensor_scalar(
                    cc(4), cc(4), 1.0 / D, None, ALU.mult))
                em.wait(s, tok_var)
                tok_sd = em.inc(s, lambda e: e.activation(cc(5), cc(4), AF.Sqrt, bias=EPS))
                em.wait(v, tok_sd)
                em.op(v, lambda e: e.reciprocal(cc(6), cc(5)))
                em.selfsync(v)
                em.op(v, lambda e: e.tensor_scalar(cc(14), cc(7), 1.0 / D, None, ALU.mult))
                em.selfsync(v)
                em.op(v, lambda e: e.tensor_tensor(cc(8), cc(14), cc(6), ALU.mult))
                em.selfsync(v)
                em.op(v, lambda e: e.tensor_scalar(cc(14), cc(6), cc(6), None, ALU.mult))
                em.selfsync(v)
                em.op(v, lambda e: e.tensor_tensor(cc(15), cc(4), cc(14), ALU.mult))
                em.selfsync(v)
                em.op(v, lambda e: e.tensor_scalar(cc(14), cc(8), cc(8), None, ALU.mult))
                em.selfsync(v)
                tok_v2 = em.inc(v, lambda e: e.tensor_tensor(cc(9), cc(15), cc(14), ALU.subtract))
                em.wait(s, tok_v2)
                tok_sd2 = em.inc(s, lambda e: e.activation(cc(10), cc(9), AF.Sqrt, bias=EPS))
                em.wait(v, tok_sd2)
                em.op(v, lambda e: e.reciprocal(cc(11), cc(10)))
                em.selfsync(v)
                em.op(v, lambda e: e.tensor_tensor(cc(14), cc(8), cc(11), ALU.mult))
                em.selfsync(v)
                em.op(v, lambda e: e.tensor_scalar(cc(12), cc(14), -1.0, None, ALU.mult))
                em.selfsync(v)
                em.op(v, lambda e: e.tensor_scalar(
                    xnt_bf[0:npart, :], src, cc(2), cc(6), ALU.subtract, ALU.mult))
                em.selfsync(v)
                em.op(v, lambda e, npart=npart: e.tensor_copy(
                    sqscr[0:npart, 0:256], xnt_bf[0:npart, 0:256]))
                em.selfsync(v)
                tok_pre = em.inc(v, lambda e, npart=npart: e.tensor_copy(
                    sqscr[0:npart, 256:512], xnt_bf[0:npart, D - 256:D]))
                return tok_pre, tok_sq

            for i in range(NT if stage not in ("p1t1", "dbgdt", "noh") else 1):
                hs_full = em.dfull("hs")
                tok_trlast = tok_xn_lastuse[0] if i > 0 else None
                for sidx in range(5):
                    if sidx < 4:
                        sl = sidx % 2
                        if free_xtok[sl] is not None:
                            em.wait(q, free_xtok[sl][0])
                            em.wait(q, free_xtok[sl][1])
                        tok_x = em.dma(q, "xt" + str(sl),
                                       lambda e, i=i, sidx=sidx, sl=sl: e.dma_start(
                            out=xtok[sl][:, :],
                            in_=x_h[1 + i * L + sidx * 128: 1 + i * L + (sidx + 1) * 128, :]))
                        src = xtok[sl][:, :]
                        npart = 128
                    else:
                        if free_xhalo is not None:
                            em.wait(q, free_xhalo[0])
                            em.wait(q, free_xhalo[1])
                        tok_x = em.dma(q, "xh", lambda e, i=i: e.dma_start(
                            out=xhalo[:, :],
                            in_=x_h[i * L: i * L + L + 2: L + 1, :]))
                        src = xhalo[:, :]
                        npart = 2
                    em.wait(v, tok_x)
                    em.wait(s, tok_x)
                    tok_pre, tok_sq = emit_ln(src, npart, tok_trlast)
                    if sidx < 4:
                        free_xtok[sidx % 2] = (tok_pre, tok_sq)
                    else:
                        free_xhalo = (tok_pre, tok_sq)
                    for k in range(G):
                        em.wait(t, tok_pre)
                        em.wait(t, free_pst[0])
                        em.inc(t, lambda e, k=k, npart=npart: e.transpose(
                            ps_t2[0][:, 0:npart],
                            xnt_bf[0:npart, k * 128:(k + 1) * 128],
                            identb[0:npart, 0:npart]))
                        tok_tr1 = em.marker(t)
                        em.wait(v, tok_tr1)
                        em.wait(v, tok_xn_lastuse[k])
                        if sidx < 4:
                            tok_ev1 = em.inc(v, lambda e, k=k, sidx=sidx: e.tensor_copy(
                                xn[k][:, sidx * 128: (sidx + 1) * 128],
                                ps_t2[0][:, :]))
                        else:
                            tok_ev1 = em.inc(v, lambda e, k=k: e.tensor_copy(
                                xn[k][:, L:L + 2],
                                ps_t2[0][:, 0:2]))
                        free_pst[0] = tok_ev1
                    if sidx < 4 and stage != "noh":
                        em.wait(q, tok_pre)
                        em.wait(q, rm_free[0])
                        for jj in range(2):
                            em.dma(q, "br0", lambda e, sidx=sidx, jj=jj: e.dma_start(
                                out=rm_sb[0:1, jj * L + sidx * 128: jj * L + (sidx + 1) * 128]
                                .rearrange("o (a b) -> o a b", a=128),
                                in_=lncol[:, 11 + jj:12 + jj]))
                    tok_trlast = ("t", em.cnt["t"])

                if stage != "noh":
                    tok_rm = em.dfull("br0")
                    em.wait(t, tok_rm)
                    em.wait(t, free_ps[4])
                    tok_bR = em.inc(t, lambda e: e.matmul(
                        ps[4][:, :], ones_col[:, :], rm_sb[0:1, 0:L],
                        start=True, stop=True))
                    em.wait(t, free_ps[5])
                    tok_bM = em.inc(t, lambda e: e.matmul(
                        ps[5][:, :], ones_col[:, :], rm_sb[0:1, L:2 * L],
                        start=True, stop=True))
                    rm_free[0] = tok_bM
                    em.wait(v, tok_bM)
                    em.wait(v, hs_full)
                    tok_h_last = None
                    for k in range(G):
                        em.wait(v, tok_xn_lastuse[k])
                        em.op(v, lambda e, k=k: e.tensor_tensor(
                            sp1[:, :], xn[k][:, 0:L], ps[4][:, :], ALU.mult))
                        em.op(v, lambda e: e.tensor_tensor(
                            sp1[:, :], sp1[:, :], ps[5][:, :], ALU.add))
                        tok_h_last = em.inc(v, lambda e, k=k: e.tensor_scalar(
                            hfm[k][:, :], sp1[:, :],
                            vecp[:, 16 + k:17 + k], vecp[:, 24 + k:25 + k],
                            ALU.mult, ALU.add))
                        em.wait(q, tok_h_last)
                        em.dma(q, "hs", lambda e, i=i, k=k: e.dma_start(
                            out=h_sp[(i * G + k) * 128:(i * G + k + 1) * 128, :],
                            in_=hfm[k][:, :]))
                    free_ps[4] = tok_h_last
                    free_ps[5] = tok_h_last

                if stage == "dbgz" and i == 0:
                    em.wait(q, ("v", em.cnt["v"]))
                    for k in range(G):
                        em.dma(q, "os", lambda e, k=k: e.dma_start(
                            out=xn_sp[k * 128:(k + 1) * 128, :], in_=xn[k][:, 0:L]))
                for sidx in range(4):
                    szl = sidx % 2
                    em.wait(s, em.dfull("zs"))
                    tok_sz = None
                    for half in range(2):
                        pidx = next_ps01()
                        em.wait(t, free_ps[pidx])
                        npass = 1
                        for zp in range(npass):
                            for k in range(G):
                                fn = (lambda e, k=k, sidx=sidx, half=half, pidx=pidx: e.matmul(
                                    ps[pidx][:, :],
                                    xn[k][:, sidx * 128: (sidx + 1) * 128],
                                    wi[k][:, D + half * 512: D + (half + 1) * 512],
                                    start=(k == 0), stop=(k == G - 1)))
                                if k == G - 1 and zp == npass - 1:
                                    tok_z = em.incm(t, fn, 'z')
                                else:
                                    em.op(t, fn)
                        em.wait(s, tok_z)
                        tok_sz = em.inc(s, lambda e, szl=szl, half=half, pidx=pidx: e.activation(
                            szslot[szl][:, half * 512:(half + 1) * 512], ps[pidx][:, :], AF.Silu))
                        free_ps[pidx] = tok_sz
                    em.wait(q, tok_sz)
                    if stage != "dbg4":
                        em.dma(q, "zs", lambda e, i=i, sidx=sidx, szl=szl: e.dma_start(
                            out=siluz_sp[(i * 4 + sidx) * 128:(i * 4 + sidx + 1) * 128, :],
                            in_=szslot[szl][:, :]))

                em.wait(s, ("g", em.cnt["g"]))
                tok_xsb_last = None
                for gch in range(G):
                    xcl = gch % 2
                    pidx = next_ps01()
                    em.wait(t, free_ps[pidx])
                    for k in range(G):
                        fn = (lambda e, k=k, gch=gch, pidx=pidx: e.matmul(
                            ps[pidx][:, :], wi[k][:, gch * 128:(gch + 1) * 128],
                            xn[k][:, 0:L], start=(k == 0), stop=(k == G - 1)))
                        if k == G - 1:
                            tok_xc = em.incm(t, fn, 'xc')
                        else:
                            em.op(t, fn)
                    em.wait(t, free_ps[4])
                    for k in range(G):
                        fn = (lambda e, k=k, gch=gch: e.matmul(
                            ps[4][:, 0:2], wi[k][:, gch * 128:(gch + 1) * 128],
                            xn[k][:, L:L + 2], start=(k == 0), stop=(k == G - 1)))
                        if k == G - 1:
                            tok_xh2 = em.incm(t, fn, 'xh2')
                        else:
                            em.op(t, fn)
                    em.wait(v, tok_xc)
                    em.wait(v, tok_xh2)
                    em.wait(v, free_xcs[xcl])
                    em.op(v, lambda e, xcl=xcl, pidx=pidx: e.tensor_copy(
                        xcs[xcl][:, 1:L + 1], ps[pidx][:, :]))
                    tok_xcs = em.inc(v, lambda e, xcl=xcl: e.tensor_copy(
                        xcs[xcl][:, 0:(L + 2):(L + 1)], ps[4][:, 0:2]))
                    free_ps[pidx] = tok_xcs
                    free_ps[4] = tok_xcs
                    em.op(v, lambda e, xcl=xcl, gch=gch: e.tensor_scalar(
                        convt[:, :], xcs[xcl][:, 0:L], convp[:, 3 * gch:3 * gch + 1],
                        None, ALU.mult))
                    em.op(v, lambda e, xcl=xcl, gch=gch: e.scalar_tensor_tensor(
                        convt[:, :], xcs[xcl][:, 1:L + 1], convp[:, 3 * gch + 1:3 * gch + 2],
                        convt[:, :], ALU.mult, ALU.add))
                    tok_conv = em.inc(v, lambda e, xcl=xcl, gch=gch: e.scalar_tensor_tensor(
                        convt[:, :], xcs[xcl][:, 2:L + 2], convp[:, 3 * gch + 2:3 * gch + 3],
                        convt[:, :], ALU.mult, ALU.add))
                    free_xcs[xcl] = tok_conv
                    em.wait(s, tok_conv)
                    tok_xs = em.inc(s, lambda e, gch=gch: e.activation(
                        xsf[gch][:, :], convt[:, :], AF.Silu))
                    em.wait(g, tok_xs)
                    tok_xsb = em.inc(g, lambda e, gch=gch: e.tensor_copy(
                        xsb[gch][:, :], xsf[gch][:, :]))
                    tok_xsb_last = tok_xsb
                    em.wait(v, tok_xs)

                em.wait(t, tok_xsb_last)
                em.wait(t, free_ps[2])
                em.wait(t, free_ps[3])
                for k in range(G):
                    em.op(t, lambda e, k=k: e.matmul(
                        ps[2][0:R + N, :], wxa[k][:, :], xsb[k][:, :],
                        start=(k == 0), stop=(k == G - 1)))
                for k in range(G):
                    fn = (lambda e, k=k: e.matmul(
                        ps[3][0:N, :], wxb[k][:, :], xsb[k][:, :],
                        start=(k == 0), stop=(k == G - 1)))
                    if k == G - 1:
                        tok_xdbl = em.incm(t, fn, 'xdbl')
                    else:
                        em.op(t, fn)
                em.wait(v, tok_xdbl)
                em.wait(v, em.dfull("br0"))
                em.wait(v, em.dfull("br1"))
                em.op(v, lambda e: e.tensor_copy(dtp_bf[:, :], ps[2][0:R, :]))
                em.op(v, lambda e, i=i: e.tensor_copy(cp_bf[i][:, :], ps[2][R:R + N, :]))
                tok_bp = em.inc(v, lambda e: e.tensor_copy(bp_f[:, :], ps[3][0:N, :]))
                free_ps[2] = tok_bp
                free_ps[3] = tok_bp

                for half in range(2):
                    em.wait(s, ("v", em.cnt["v"]))
                    em.wait(g, ("v", em.cnt["v"]))
                    em.wait(v, ("g", em.cnt["g"]))
                    tok_dt_done = [None] * 4
                    for gg in range(4):
                        gch = half * 4 + gg
                        pidx = next_ps01()
                        em.wait(t, free_ps[pidx])
                        em.wait(t, tok_bp)
                        tok_dtm = em.incm(t, site='dtm', f=lambda e, gch=gch, pidx=pidx: e.matmul(
                            ps[pidx][:, :], wdt[:, gch * 128:(gch + 1) * 128], dtp_bf[:, :],
                            start=True, stop=True))
                        em.wait(v, tok_dtm)
                        tok_r1 = em.inc(v, lambda e, gch=gch, pidx=pidx: e.tensor_scalar(
                            sp3[:, :], ps[pidx][:, :], vecp[:, gch:gch + 1], 0.0,
                            ALU.add, ALU.max))
                        em.wait(s, tok_dtm)
                        em.wait(s, tok_r1)
                        em.wait(s, prev_tok_dt[0])
                        em.op(s, lambda e, gch=gch, pidx=pidx: e.activation(
                            sp1[:, :], ps[pidx][:, :], AF.Abs, bias=vecp[:, gch:gch + 1]))
                        em.op(s, lambda e: e.activation(sp2[:, :], sp1[:, :], AF.Exp, scale=-1.0))
                        tok_ln1p = em.inc(s, lambda e: e.activation(
                            sp1[:, :], sp2[:, :], AF.Ln, bias=1.0))
                        free_ps[pidx] = tok_ln1p
                        em.wait(v, tok_ln1p)
                        em.wait(v, tok_r1)
                        tok_dt = em.inc(v, lambda e, gg=gg: e.scalar_tensor_tensor(
                            dt_h[gg][:, :], sp1[:, :], 1e-5, sp3[:, :], ALU.add, ALU.add))
                        tok_dt_done[gg] = tok_dt
                        prev_tok_dt[0] = tok_dt
                        cl = gg % 2
                        em.wait(v, free_cdt[cl])
                        em.wait(v, cdt_carry_done[gch])
                        tok_cdtscan = em.inc(v, lambda e, gg=gg, gch=gch, cl=cl: e.tensor_tensor_scan(
                            cdt_s[cl][:, :], ones_row[:, :], dt_h[gg][:, :],
                            cdtcarry[gch][:, 0:1], ALU.mult, ALU.add))
                        em.wait(s, tok_cdtscan)
                        tok_cdt = em.inc(s, lambda e, gch=gch, cl=cl: e.copy(
                            cdtcarry[gch][:, 0:1], cdt_s[cl][:, L - 1:L]))
                        cdt_carry_done[gch] = tok_cdt
                        em.wait(q, tok_cdtscan)
                        em.wait(q, tok_cdt)
                        if stage in ("dbgdt", "dbg2", "dbg3", "dbg4"):
                            em.dma(q, "cs", lambda e, i=i, gch=gch, gg=gg: e.dma_start(
                                out=cdt_sp[(i * G + gch) * 128:(i * G + gch + 1) * 128, :],
                                in_=dt_h[gg][:, :]))
                        else:
                            em.dma(q, "cs", lambda e, i=i, gch=gch, cl=cl: e.dma_start(
                                out=cdt_sp[(i * G + gch) * 128:(i * G + gch + 1) * 128, :],
                                in_=cdt_s[cl][:, :]))
                        free_cdt[cl] = em.dfull("cs")
                        em.wait(g, tok_dt)
                        em.inc(g, lambda e, gg=gg, gch=gch: e.tensor_tensor(
                            dtxs_h[gg][:, :], dt_h[gg][:, :], xsf[gch][:, :], ALU.mult))
                    tok_dtxs_all = ("g", em.cnt["g"])

                    em.wait(q, tok_bp)
                    for n in range(N):
                        brl = n % 2
                        em.wait(q, free_bprow[brl])
                        tok_row = em.dma(q, "br" + str(brl),
                                         lambda e, n=n, brl=brl: e.dma_start(
                            out=bp_row[brl][:, :], in_=bp_f[n:n + 1, :]))
                        pb = 4 + (n % 2)
                        em.wait(t, free_ps[pb])
                        em.wait(t, tok_row)
                        tok_bc = em.incm(t, site='bc', f=lambda e, brl=brl, pb=pb: e.matmul(
                            ps[pb][:, :], ones_col[:, :], bp_row[brl][:, :],
                            start=True, stop=True))
                        free_bprow[brl] = tok_bc
                        if n % 2 == 1:
                            em.wait(s, tok_bc)
                            em.wait(s, free_bpb[0])
                            tok_evb = em.inc(s, lambda e, pb=pb: e.copy(
                                bpb[:, :], ps[pb][:, :]))
                            free_ps[pb] = tok_evb
                        for gg in range(4):
                            gch = half * 4 + gg
                            sl2 = (n * 4 + gg) % 2
                            em.wait(s, tok_dt_done[gg])
                            em.wait(s, free_sa[sl2])
                            tok_A = em.inc(s, lambda e, gg=gg, gch=gch, n=n, sl2=sl2: e.activation(
                                sa_s[sl2][:, :], dt_h[gg][:, :], AF.Exp,
                                scale=nega[:, gch * N + n: gch * N + n + 1]))
                            if n % 2 == 1:
                                em.wait(g, tok_evb)
                                em.wait(g, free_sbx[sl2])
                                tok_Bx = em.inc(g, lambda e, gg=gg, sl2=sl2: e.tensor_tensor(
                                    sbx_s[sl2][:, :], bpb[:, :], dtxs_h[gg][:, :], ALU.mult))
                                if gg == 3:
                                    free_bpb[0] = tok_Bx
                            else:
                                em.wait(v, tok_bc)
                                em.wait(v, tok_dtxs_all)
                                em.wait(v, free_sbx[sl2])
                                tok_Bx = em.inc(v, lambda e, gg=gg, pb=pb, sl2=sl2: e.tensor_tensor(
                                    sbx_s[sl2][:, :], ps[pb][:, :], dtxs_h[gg][:, :], ALU.mult))
                                if gg == 3:
                                    free_ps[pb] = tok_Bx
                            em.wait(v, tok_Bx)
                            em.wait(v, tok_A)
                            em.wait(v, free_ss[sl2])
                            em.wait(v, free_ss_scpy[sl2])
                            em.wait(v, s_carry_done[gch * N + n])
                            tok_scan = em.inc(v, lambda e, sl2=sl2, gch=gch, n=n: e.tensor_tensor_scan(
                                ss_s[sl2][:, :], sa_s[sl2][:, :], sbx_s[sl2][:, :],
                                scarry[gch * N + n][:, 0:1], ALU.mult, ALU.add))
                            free_sa[sl2] = tok_scan
                            free_sbx[sl2] = tok_scan
                            em.wait(s, tok_scan)
                            tok_scpy = em.inc(s, lambda e, sl2=sl2, gch=gch, n=n: e.copy(
                                scarry[gch * N + n][:, 0:1], ss_s[sl2][:, L - 1:L]))
                            s_carry_done[gch * N + n] = tok_scpy
                            em.wait(g, tok_scan)
                            if n == 0:
                                tok_acc = em.inc(g, lambda e, gg=gg, sl2=sl2: e.tensor_copy(
                                    ssum_h[gg][:, :], ss_s[sl2][:, :]))
                            else:
                                tok_acc = em.inc(g, lambda e, gg=gg, sl2=sl2: e.tensor_tensor(
                                    ssum_h[gg][:, :], ssum_h[gg][:, :], ss_s[sl2][:, :], ALU.add))
                            free_ss[sl2] = tok_acc
                            free_ss_scpy[sl2] = tok_scpy
                            last_scan_tok = tok_acc

                    tok_pool_done = ("g", em.cnt["g"])
                    for gg in range(4):
                        gch = half * 4 + gg
                        pidx = next_ps01()
                        em.wait(t, free_ps[pidx])
                        tok_Cm = em.incm(t, site='Cm', f=lambda e, gch=gch, pidx=pidx, i=i: e.matmul(
                            ps[pidx][:, :], wc[:, gch * 128:(gch + 1) * 128], cp_bf[i][:, :],
                            start=True, stop=True))
                        em.wait(v, tok_Cm)
                        em.wait(v, tok_pool_done)
                        em.op(v, lambda e, gg=gg, pidx=pidx: e.tensor_tensor(
                            sp1[:, :], ps[pidx][:, :], ssum_h[gg][:, :], ALU.mult))
                        ypl = gg % 2
                        em.wait(v, free_yps[ypl])
                        if stage in ("dbg2", "dbg3", "dbg4"):
                            tok_yp = em.inc(v, lambda e, gch=gch, ypl=ypl: e.tensor_copy(
                                yps[ypl][:, :], xsf[gch][:, :]))
                        else:
                            tok_yp = em.inc(v, lambda e, gch=gch, ypl=ypl: e.scalar_tensor_tensor(
                                yps[ypl][:, :], xsf[gch][:, :], vecp[:, 8 + gch:9 + gch],
                                sp1[:, :], ALU.mult, ALU.add))
                        free_ps[pidx] = tok_yp
                        em.wait(q, tok_yp)
                        em.dma(q, "ys", lambda e, i=i, gch=gch, ypl=ypl: e.dma_start(
                            out=yp_sp[(i * G + gch) * 128:(i * G + gch + 1) * 128, :],
                            in_=yps[ypl][:, :]))
                        free_yps[ypl] = em.dfull("ys")

                for k in range(G):
                    tok_xn_lastuse[k] = ("t", em.cnt["t"])

            em.wait(v, ("s", em.cnt["s"]))
            tok_pt1 = None
            for gch in range(G):
                fn_pt = (lambda e, gch=gch: e.tensor_scalar(
                    sqscr[:, gch * N:(gch + 1) * N], nega[:, gch * N:(gch + 1) * N],
                    cdtcarry[gch][:, 0:1], None, ALU.mult))
                if gch == G - 1:
                    tok_pt1 = em.inc(v, fn_pt)
                else:
                    em.op(v, fn_pt)
            em.wait(s, tok_pt1)
            tok_pt = em.inc(s, lambda e: e.activation(
                sqscr[:, 0:G * N], sqscr[:, 0:G * N], AF.Exp))
            tok_cat = None
            for j in range(G * N):
                fn_cat = (lambda e, j=j: e.copy(scarry_cat[:, j:j + 1], scarry[j][:, 0:1]))
                if j == G * N - 1:
                    tok_cat = em.inc(s, fn_cat)
                else:
                    em.op(s, fn_cat)
            em.wait(q, tok_pt)
            em.wait(q, tok_cat)
            em.wait(q, last_scan_tok)
            em.dma(q, "st", lambda e: e.dma_start(out=st_in[:, 0:G * N], in_=scarry_cat[:, :]))
            em.dma(q, "st", lambda e: e.dma_start(
                out=st_in[:, G * N:2 * G * N], in_=sqscr[:, 0:G * N]))
            tok_sti = em.dfull("st")
            em.wait(g, tok_sti)
            ccnt[0] += 1
            em.op(g, lambda e: e.collective_compute(
                "AllGather", ALU.bypass, replica_groups=[list(range(W))],
                ins=[st_in.opt()], outs=[st_out.opt()]).then_inc(csem, 1))
            state_ag_cnt = ccnt[0]

            em.barrier(light=("g",))
            if stage in ("p1", "p1t1", "dbgdt", "dbg2", "dbg3", "dbg4", "noh"):
                for d in DSEMS:
                    if em.dcnt[d]:
                        em.wait(q, em.dfull(d))
                em.marker(q)
            em.run(block, dict(sems, c=csem))
        if stage in ("p1", "p1t1", "dbgdt", "dbg2", "dbg3", "dbg4", "noh"):
            return nc

        with ExitStack() as f_scope:
            fe = f_scope.enter_context
            wo = [fe(nc.sbuf_tensor(f"wo{k}", [128, D], BF16)) for k in range(G)]
            rwt = [fe(nc.sbuf_tensor(f"rwt{k}", [128, NE], BF16)) for k in range(G)]
            hf = [fe(nc.sbuf_tensor(f"hf{k}", [128, L], BF16)) for k in range(G)]
            ew1t2 = [[fe(nc.sbuf_tensor(f"ew1t{sl}_{k}", [128, DFF], BF16))
                      for k in range(G)] for sl in range(2)]
            ew2t = [fe(nc.sbuf_tensor(f"ew2t{j}", [128, D], BF16)) for j in range(FC)]
            h1s = [fe(nc.sbuf_tensor(f"h1s{j}", [128, L], BF16)) for j in range(FC)]
            acc = [fe(nc.sbuf_tensor(f"acc{si}", [128, D], F32)) for si in range(4)]
            gate = fe(nc.sbuf_tensor("gate", [128, 4 * NE], F32))
            gsc = [fe(nc.sbuf_tensor(f"gsc{j}", [128, NE], F32)) for j in range(3)]
            gcol = fe(nc.sbuf_tensor("gcol", [128, 6], F32))
            zerocol = fe(nc.sbuf_tensor("zerocol", [128, 1], F32))
            sallb = fe(nc.sbuf_tensor("sallb", [128, 2 * G * N], F32))
            stmp = [fe(nc.sbuf_tensor(f"stmp{j}", [128, G * N], F32)) for j in range(2)]
            cdtr = [fe(nc.sbuf_tensor(f"cdtr{j}", [128, L], F32)) for j in range(2)]
            pex = [fe(nc.sbuf_tensor(f"pex{j}", [128, L], F32)) for j in range(2)]
            corr = fe(nc.sbuf_tensor("corr", [128, L], F32))
            ypr = [fe(nc.sbuf_tensor(f"ypr{j}", [128, L], F32)) for j in range(2)]
            ybf = [fe(nc.sbuf_tensor(f"ybf{k}", [128, L], BF16)) for k in range(G)]
            fsp1 = fe(nc.sbuf_tensor("fsp1", [128, L], F32))
            szr1 = fe(nc.sbuf_tensor("szr1", [128, L], F32))
            xr1 = fe(nc.sbuf_tensor("xr1", [128, L], F32))
            outk = [fe(nc.sbuf_tensor(f"outk{j}", [128, D], F32)) for j in range(2)]

            for k in range(G):
                em.dma(g, "gw", lambda e, k=k: e.dma_start(
                    out=wo[k][:, :], in_=woutT_h[k * 128:(k + 1) * 128, :]))
            for k in range(G):
                em.dma(g, "gw", lambda e, k=k: e.dma_start(
                    out=rwt[k][:, :], in_=rwT_h[k * 128:(k + 1) * 128, :]))
            tok_wf = em.dfull("gw")
            em.wait(t, tok_wf)

            em.op(q, lambda e, c=state_ag_cnt: e.wait_ge(csem, c))
            em.op(v, lambda e: e.memset(s_in[:, :], 0.0))
            prev_rank_use = None
            for j in range(W):
                em.wait(q, prev_rank_use)
                tok_rj = em.dma(q, "st", lambda e, j=j: e.dma_start(
                    out=sallb[:, :], in_=st_out[j * 128:(j + 1) * 128, :]))
                em.wait(v, tok_rj)
                em.op(v, lambda e: e.tensor_tensor(
                    stmp[0][:, :], sallb[:, 128:256], s_in[:, :], ALU.mult))
                em.op(v, lambda e: e.tensor_tensor(
                    stmp[1][:, :], stmp[0][:, :], sallb[:, 0:128], ALU.add))
                em.op(v, lambda e: e.tensor_tensor(
                    stmp[0][:, :], stmp[1][:, :], s_in[:, :], ALU.subtract))
                prev_rank_use = em.inc(v, lambda e, j=j: e.scalar_tensor_tensor(
                    s_in[:, :], stmp[0][:, :], ucoef[:, j:j + 1], s_in[:, :], ALU.mult, ALU.add))
            tok_sin = em.inc(v, lambda e: e.tensor_copy(stmp[0][:, 0:1], s_in[:, 0:1]))
            em.op(v, lambda e: e.memset(zerocol[:, 0:1], 0.0))
            em.selfsync(v)

            em.op(q, lambda e, c=wag_cnt: e.wait_ge(csem, c))
            for rr in range(W):
                em.dma(q, "asm", lambda e, rr=rr: e.dma_start(
                    out=sw1T_lin[:, rr * (DFF // W):(rr + 1) * (DFF // W)],
                    in_=ag_s1[rr * D:(rr + 1) * D, :]))
                em.dma(q, "asm", lambda e, rr=rr: e.dma_start(
                    out=sw2T_lin[:, rr * (D // W):(rr + 1) * (D // W)],
                    in_=ag_s2[rr * DFF:(rr + 1) * DFF, :]))
            tok_asm = em.dfull("asm")

            free_psf = [None] * 6
            free_hf = [None] * G
            free_ew1sl = [None, None]
            tok_w1sl = [None, None]
            free_ew2 = None
            free_h1 = [None] * FC
            free_cdtr = [None, None]
            free_fin = [None, None]
            free_pex = [None, None]
            free_ypr = [None, None]
            free_outk = [None, None]
            free_acc = [None] * 4
            tok_ybf_lastuse = [None] * G

            for i in range(NT):
                for k in range(G):
                    em.wait(q, free_hf[k])
                    em.dma(q, "hf", lambda e, i=i, k=k: e.dma_start(
                        out=hf[k][:, :], in_=h_sp[(i * G + k) * 128:(i * G + k + 1) * 128, :]))
                tok_hf_full = em.dfull("hf")
                if dbg:
                    em.wait(v, em.dfull("dbg"))
                for sidx in range(4):
                    rb = sidx % 4
                    em.wait(t, tok_hf_full)
                    em.wait(t, free_psf[rb])
                    for k in range(G):
                        fn = (lambda e, k=k, sidx=sidx, rb=rb: e.matmul(
                            ps[rb][:, 0:NE], hf[k][:, sidx * 128:(sidx + 1) * 128],
                            rwt[k][:, :], start=(k == 0), stop=(k == G - 1)))
                        if k == G - 1:
                            tok_rl = em.incm(t, fn, 'rl')
                        else:
                            em.op(t, fn)
                    em.wait(v, tok_rl)
                    tok_r0 = em.inc(v, lambda e, rb=rb: e.tensor_reduce(
                        gcol[:, 0:1], ps[rb][:, 0:NE], AX.X, ALU.max))
                    em.wait(s, tok_r0)
                    tok_rb0 = em.inc(s, lambda e: e.copy(gcol[:, 4:5], gcol[:, 0:1]))
                    em.wait(v, tok_rb0)
                    tok_lg = em.inc(v, lambda e, rb=rb: e.tensor_scalar(
                        gsc[0][:, :], ps[rb][:, 0:NE], gcol[:, 0:1], None, ALU.subtract))
                    em.wait(s, tok_lg)
                    tok_ex = em.inc(s, lambda e: e.activation(gsc[1][:, :], gsc[0][:, :], AF.Exp))
                    em.wait(v, tok_ex)
                    tok_r1 = em.inc(v, lambda e: e.tensor_reduce(
                        gcol[:, 1:2], gsc[1][:, :], AX.X, ALU.add))
                    em.wait(s, tok_r1)
                    tok_rb1 = em.inc(s, lambda e: e.copy(gcol[:, 4:5], gcol[:, 1:2]))
                    em.wait(v, tok_rb1)
                    tok_r2 = em.inc(v, lambda e: e.reciprocal(gcol[:, 2:3], gcol[:, 1:2]))
                    em.wait(s, tok_r2)
                    tok_rb2 = em.inc(s, lambda e: e.copy(gcol[:, 4:5], gcol[:, 2:3]))
                    em.wait(v, tok_rb2)
                    em.op(v, lambda e: e.tensor_scalar(
                        gsc[0][:, :], gsc[1][:, :], gcol[:, 2:3], None, ALU.mult))
                    em.selfsync(v)
                    tok_r3 = em.inc(v, lambda e: e.tensor_reduce(
                        gcol[:, 3:4], gsc[0][:, :], AX.X, ALU.max))
                    em.wait(s, tok_r3)
                    tok_rb3 = em.inc(s, lambda e: e.copy(gcol[:, 4:5], gcol[:, 3:4]))
                    em.wait(v, tok_rb3)
                    em.op(v, lambda e: e.tensor_scalar(
                        gsc[1][:, :], gsc[0][:, :], gcol[:, 3:4], None, ALU.is_ge))
                    em.selfsync(v)
                    em.op(v, lambda e: e.tensor_tensor_scan(
                        gsc[2][:, :], ones_row[:, 0:NE], gsc[1][:, :],
                        zerocol[:, 0:1], ALU.mult, ALU.add))
                    em.selfsync(v)
                    em.op(v, lambda e: e.tensor_tensor(
                        gsc[2][:, :], gsc[2][:, :], gsc[1][:, :], ALU.subtract))
                    em.selfsync(v)
                    em.op(v, lambda e: e.tensor_scalar(
                        gsc[2][:, :], gsc[2][:, :], 0.0, None, ALU.is_equal))
                    em.selfsync(v)
                    em.op(v, lambda e: e.tensor_tensor(
                        gsc[1][:, :], gsc[1][:, :], gsc[2][:, :], ALU.mult))
                    em.selfsync(v)
                    tok_gate1 = em.inc(v, lambda e, sidx=sidx: e.tensor_scalar(
                        gate[:, sidx * NE:(sidx + 1) * NE], gsc[1][:, :], gcol[:, 3:4],
                        None, ALU.mult))
                    free_psf[rb] = tok_gate1
                for si in range(4):
                    em.wait(v, free_acc[si])
                    em.inc(v, lambda e, si=si: e.memset(acc[si][:, :], 0.0))
                tok_gates = ("v", em.cnt["v"])
                if dbg:
                    em.wait(q, tok_gates)
                    em.dma(q, "dbg", lambda e, i=i: e.dma_start(
                        out=gate_sp[i * 128:(i + 1) * 128, :], in_=gate[:, :]))

                def w1src(ei):
                    return ag_ew1[ei * D:(ei + 1) * D, :] if ei < NE else sw1T_lin

                def w1_load(ei):
                    sl = ei % 2
                    em.wait(q, free_ew1sl[sl])
                    for k in range(G):
                        em.dma(q, "qld" + str(sl), lambda e, k=k, ei=ei: e.dma_start(
                            out=ew1t2[ei % 2][k][:, :],
                            in_=w1src(ei)[k * 128:(k + 1) * 128, :]))
                    tok_w1sl[sl] = em.dfull("qld" + str(sl))

                if i == 0:
                    em.wait(q, tok_asm)
                if PREFETCH:
                    w1_load(0)
                for ei in range(NE + 1):
                    sl = ei % 2
                    ew1t = ew1t2[sl]
                    if ei < NE:
                        src2 = ag_ew2[ei * DFF:(ei + 1) * DFF, :]
                    else:
                        src2 = sw2T_lin
                    if PREFETCH:
                        if ei < NE:
                            w1_load(ei + 1)
                    else:
                        w1_load(ei)
                    tok_w1 = tok_w1sl[sl]
                    em.wait(q, free_ew2)
                    for fc in range(FC):
                        em.dma(q, "e2", lambda e, fc=fc, src2=src2: e.dma_start(
                            out=ew2t[fc][:, :], in_=src2[fc * 128:(fc + 1) * 128, :]))
                    tok_w2 = em.dfull("e2")
                    if dbg and i == 0 and ei in (0, NE):
                        em.wait(q, tok_w1)
                        em.wait(q, tok_w2)
                        d1, d2 = (e1_sp, e2_sp) if ei == 0 else (s1_sp, s2_sp)
                        for k in range(G):
                            em.dma(q, "dbg", lambda e, k=k, d1=d1, ew1t=ew1t: e.dma_start(
                                out=d1[k * 128:(k + 1) * 128, :], in_=ew1t[k][:, :]))
                        for fc in range(FC):
                            em.dma(q, "dbg", lambda e, fc=fc, d2=d2: e.dma_start(
                                out=d2[fc * 128:(fc + 1) * 128, :], in_=ew2t[fc][:, :]))
                        em.wait(t, em.dfull("dbg"))
                    tok_h1 = [None] * FC
                    for fc in range(FC):
                        pidx = fc % 4
                        em.wait(t, free_psf[pidx])
                        em.wait(t, tok_w1)
                        if ei == 0:
                            em.wait(t, tok_hf_full)
                        for k in range(G):
                            fn = (lambda e, k=k, fc=fc, pidx=pidx, ew1t=ew1t: e.matmul(
                                ps[pidx][:, :], ew1t[k][:, fc * 128:(fc + 1) * 128],
                                hf[k][:, :], start=(k == 0), stop=(k == G - 1)))
                            if k == G - 1:
                                tok_m1 = em.incm(t, fn, 'm1')
                            else:
                                em.op(t, fn)
                        em.wait(s, tok_m1)
                        em.wait(s, free_h1[fc])
                        tok_h1[fc] = em.inc(s, lambda e, pidx=pidx, fc=fc: e.activation(
                            h1s[fc][:, :], ps[pidx][:, :], AF.Silu))
                        free_psf[pidx] = tok_h1[fc]
                    free_ew1sl[sl] = ("t", em.cnt["t"])
                    tok_m2_half = [None, None]
                    for ht in range(2):
                        for bank in range(2, 6):
                            em.wait(t, free_psf[bank])
                        em.wait(t, tok_w2)
                        for fc in range(FC):
                            em.wait(t, tok_h1[fc])
                            for si2 in range(2):
                                si = 2 * ht + si2
                                for dh in range(2):
                                    bank = 2 + si2 * 2 + dh
                                    fn = (lambda e, fc=fc, si=si, dh=dh, bank=bank: e.matmul(
                                        ps[bank][:, :],
                                        h1s[fc][:, si * 128:(si + 1) * 128],
                                        ew2t[fc][:, dh * 512:(dh + 1) * 512],
                                        start=(fc == 0), stop=(fc == FC - 1)))
                                    if fc == FC - 1 and si2 == 1 and dh == 1:
                                        tok_m2_half[ht] = em.incm(t, fn, 'm2')
                                    else:
                                        em.op(t, fn)
                        em.wait(v, tok_m2_half[ht])
                        if ei == 0:
                            em.wait(v, tok_gates)
                        tok_ga = None
                        for si2 in range(2):
                            si = 2 * ht + si2
                            for dh in range(2):
                                bank = 2 + si2 * 2 + dh
                                if ei < NE:
                                    fn = (lambda e, si=si, dh=dh, bank=bank, ei=ei:
                                          e.scalar_tensor_tensor(
                                              acc[si][:, dh * 512:(dh + 1) * 512],
                                              ps[bank][:, :],
                                              gate[:, si * NE + ei: si * NE + ei + 1],
                                              acc[si][:, dh * 512:(dh + 1) * 512],
                                              ALU.mult, ALU.add))
                                else:
                                    fn = (lambda e, si=si, dh=dh, bank=bank: e.tensor_tensor(
                                        acc[si][:, dh * 512:(dh + 1) * 512],
                                        acc[si][:, dh * 512:(dh + 1) * 512],
                                        ps[bank][:, :], ALU.add))
                                tok_ga = em.inc(v, fn)
                        for bank in range(2, 6):
                            free_psf[bank] = tok_ga
                    free_ew2 = ("t", em.cnt["t"])
                    for fc in range(FC):
                        free_h1[fc] = ("t", em.cnt["t"])

                if dbg:
                    em.wait(q, tok_ga)
                    for si in range(4):
                        em.dma(q, "dbg", lambda e, i=i, si=si: e.dma_start(
                            out=acc_sp[(i * 4 + si) * 128:(i * 4 + si + 1) * 128, :],
                            in_=acc[si][:, :]))
                em.wait(s, tok_sin)
                em.wait(v, tok_sin)
                tok_y = None
                for gch in range(G):
                    cl = gch % 2
                    em.wait(q, free_cdtr[cl])
                    tok_cd = em.dma(q, "cr" + str(cl),
                                    lambda e, i=i, gch=gch, cl=cl: e.dma_start(
                        out=cdtr[cl][:, :],
                        in_=cdt_sp[(i * G + gch) * 128:(i * G + gch + 1) * 128, :]))
                    tok_cr = None
                    for n in range(N):
                        pxl = n % 2
                        em.wait(s, tok_cd)
                        em.wait(s, free_pex[pxl])
                        tok_P = em.inc(s, lambda e, cl=cl, gch=gch, n=n, pxl=pxl: e.activation(
                            pex[pxl][:, :], cdtr[cl][:, :], AF.Exp,
                            scale=nega[:, gch * N + n: gch * N + n + 1]))
                        em.wait(v, tok_P)
                        if n == 0:
                            tok_cr = em.inc(v, lambda e, gch=gch, n=n, pxl=pxl: e.tensor_scalar(
                                corr[:, :], pex[pxl][:, :],
                                s_in[:, gch * N + n: gch * N + n + 1], None, ALU.mult))
                        else:
                            tok_cr = em.inc(v, lambda e, gch=gch, n=n, pxl=pxl:
                                            e.scalar_tensor_tensor(
                                                corr[:, :], pex[pxl][:, :],
                                                s_in[:, gch * N + n: gch * N + n + 1],
                                                corr[:, :], ALU.mult, ALU.add))
                        free_pex[pxl] = tok_cr
                    free_cdtr[cl] = tok_cr
                    pidx = gch % 2
                    em.wait(t, free_psf[pidx])
                    tok_C2 = em.incm(t, site='C2', f=lambda e, gch=gch, i=i, pidx=pidx: e.matmul(
                        ps[pidx][:, :], wc[:, gch * 128:(gch + 1) * 128], cp_bf[i][:, :],
                        start=True, stop=True))
                    ypl = gch % 2
                    em.wait(q, free_ypr[ypl])
                    tok_yr = em.dma(q, "yr" + str(ypl),
                                    lambda e, i=i, gch=gch, ypl=ypl: e.dma_start(
                        out=ypr[ypl][:, :],
                        in_=yp_sp[(i * G + gch) * 128:(i * G + gch + 1) * 128, :]))
                    em.wait(v, tok_C2)
                    em.wait(v, tok_yr)
                    em.op(v, lambda e, pidx=pidx: e.tensor_tensor(
                        fsp1[:, :], ps[pidx][:, :], corr[:, :], ALU.mult))
                    em.wait(v, tok_ybf_lastuse[gch])
                    tok_y = em.inc(v, lambda e, gch=gch, ypl=ypl: e.tensor_tensor(
                        ybf[gch][:, :], fsp1[:, :], ypr[ypl][:, :], ALU.add))
                    free_psf[pidx] = tok_y
                    free_ypr[ypl] = tok_y

                for sidx in range(4):
                    okl = sidx % 2
                    em.wait(v, free_outk[okl])
                    tok_out = None
                    for dh in range(2):
                        em.wait(q, free_fin[0])
                        em.dma(q, "fin", lambda e, i=i, sidx=sidx, dh=dh: e.dma_start(
                            out=szr1[:, :],
                            in_=siluz_sp[(i * 4 + sidx) * 128:(i * 4 + sidx + 1) * 128,
                                         dh * 512:(dh + 1) * 512]))
                        em.dma(q, "fin", lambda e, i=i, sidx=sidx, dh=dh: e.dma_start(
                            out=xr1[:, :],
                            in_=x_h[1 + i * L + sidx * 128: 1 + i * L + (sidx + 1) * 128,
                                    dh * 512:(dh + 1) * 512]))
                        tok_fin = em.dfull("fin")
                        pidx = 2 + dh
                        em.wait(t, free_psf[pidx])
                        em.wait(t, tok_y)
                        for k in range(G):
                            fn = (lambda e, k=k, sidx=sidx, dh=dh, pidx=pidx: e.matmul(
                                ps[pidx][:, :], ybf[k][:, sidx * 128:(sidx + 1) * 128],
                                wo[k][:, dh * 512:(dh + 1) * 512],
                                start=(k == 0), stop=(k == G - 1)))
                            if k == G - 1:
                                tok_ao = em.incm(t, fn, 'ao')
                            else:
                                em.op(t, fn)
                        em.wait(v, tok_ao)
                        em.wait(v, tok_fin)
                        em.op(v, lambda e, okl=okl, dh=dh, pidx=pidx: e.tensor_tensor(
                            outk[okl][:, dh * 512:(dh + 1) * 512], ps[pidx][:, :],
                            szr1[:, :], ALU.mult))
                        em.op(v, lambda e, okl=okl, dh=dh: e.tensor_tensor(
                            outk[okl][:, dh * 512:(dh + 1) * 512],
                            outk[okl][:, dh * 512:(dh + 1) * 512],
                            xr1[:, :], ALU.add))
                        tok_out = em.inc(v, lambda e, okl=okl, dh=dh, sidx=sidx: e.tensor_tensor(
                            outk[okl][:, dh * 512:(dh + 1) * 512],
                            outk[okl][:, dh * 512:(dh + 1) * 512],
                            acc[sidx][:, dh * 512:(dh + 1) * 512], ALU.add))
                        free_psf[pidx] = tok_out
                        free_fin[0] = tok_out
                    free_acc[sidx] = tok_out
                    em.wait(q, tok_out)
                    em.dma(q, "os", lambda e, i=i, sidx=sidx, okl=okl: e.dma_start(
                        out=out_h[i * L + sidx * 128: i * L + (sidx + 1) * 128, :],
                        in_=outk[okl][:, :]))
                    free_outk[okl] = em.dfull("os")
                for k in range(G):
                    free_hf[k] = ("t", em.cnt["t"])
                    tok_ybf_lastuse[k] = ("t", em.cnt["t"])

            em.wait(q, em.dfull("os"))
            em.marker(q)
            em.run(block, dict(sems, c=csem))

    return nc


def host_inputs(inputs):
    x = np.asarray(inputs["L0_attn_x"], dtype=np.float32)
    W_in = np.asarray(inputs["L0_attn_W_in"], dtype=np.float32)
    conv_w = np.asarray(inputs["L0_attn_conv_w"], dtype=np.float32)
    W_x = np.asarray(inputs["L0_attn_W_x"], dtype=np.float32)
    W_dt = np.asarray(inputs["L0_attn_W_dt"], dtype=np.float32)
    b_dt = np.asarray(inputs["L0_attn_b_dt"], dtype=np.float32)
    A_log = np.asarray(inputs["L0_attn_A_log"], dtype=np.float32)
    D_param = np.asarray(inputs["L0_attn_D_param"], dtype=np.float32)
    W_C = np.asarray(inputs["L0_attn_W_C"], dtype=np.float32)
    W_out = np.asarray(inputs["L0_attn_W_out"], dtype=np.float32)
    sw1 = np.asarray(inputs["L0_ffn_shared_w1"], dtype=np.float32)
    sw2 = np.asarray(inputs["L0_ffn_shared_w2"], dtype=np.float32)
    ew1 = np.asarray(inputs["L0_ffn_expert_w1"], dtype=np.float32)
    ew2 = np.asarray(inputs["L0_ffn_expert_w2"], dtype=np.float32)
    rw = np.asarray(inputs["L0_ffn_router_w"], dtype=np.float32)
    gam = np.asarray(inputs["L0_ffn_ln_gamma"], dtype=np.float32)
    bet = np.asarray(inputs["L0_ffn_ln_beta"], dtype=np.float32)

    xpad = np.zeros((B, T + 2, D), dtype=np.float32)
    xpad[:, 1:-1, :] = x

    winT = np.ascontiguousarray(W_in.T)
    woutT = np.ascontiguousarray(W_out.T)
    wdtT = np.ascontiguousarray(W_dt.T)
    wcT = np.ascontiguousarray(W_C.T)
    wxa = np.ascontiguousarray(np.concatenate([W_x[0:R], W_x[R + N:R + 2 * N]], axis=0).T)
    wxb = np.ascontiguousarray(W_x[R:R + N].T)
    rwT = np.ascontiguousarray(rw.T)
    alog_pack = np.ascontiguousarray(
        A_log.reshape(G, 128, N).transpose(1, 0, 2).reshape(128, G * N))
    conv_pack = np.ascontiguousarray(
        conv_w.reshape(G, 128, 3).transpose(1, 0, 2).reshape(128, G * 3))
    vec_pack = np.ascontiguousarray(np.concatenate([
        b_dt.reshape(G, 128).T, D_param.reshape(G, 128).T,
        gam.reshape(G, 128).T, bet.reshape(G, 128).T], axis=1))
    identb = np.eye(128, dtype=ml_dtypes.bfloat16)

    in_maps = []
    for c in range(W):
        b, ch = c // 4, c % 4
        ucoef = np.zeros((128, 8), dtype=np.float32)
        for j in range(W):
            if j // 4 == b and j % 4 < ch:
                ucoef[:, j] = 1.0
        in_maps.append({
            "x_slice": np.ascontiguousarray(xpad[b, ch * TC: (ch + 1) * TC + 2, :]),
            "w_inT": winT, "w_outT": woutT, "w_dtT": wdtT, "w_cT": wcT,
            "w_xTA": wxa, "w_xTB": wxb, "r_wT": rwT,
            "alog_pack": alog_pack, "conv_pack": conv_pack, "vec_pack": vec_pack,
            "ident_bf": identb, "ident_f": np.eye(128, dtype=np.float32), "ucoef": ucoef,
            "qw1": np.ascontiguousarray(ew1[c]),
            "qw2": np.ascontiguousarray(ew2[c]),
            "qs1": np.ascontiguousarray(sw1[c * (DFF // W):(c + 1) * (DFF // W)]),
            "qs2": np.ascontiguousarray(sw2[c * (D // W):(c + 1) * (D // W)]),
        })
    return in_maps


def kernel(**inputs):
    in_maps = host_inputs(inputs)
    nc = build_nc(dbg=False)
    res = run_bass_kernel_spmd(nc, in_maps, core_ids=list(range(W)))
    out = np.empty((B, T, D), dtype=np.float32)
    for c in range(W):
        b, ch = c // 4, c % 4
        out[b, ch * TC:(ch + 1) * TC, :] = np.asarray(res.results[c]["out"])
    return out


# version 12
# speedup vs baseline: 1.0690x; 1.0690x over previous
import numpy as np
import ml_dtypes
import concourse.bass as bass
import concourse.mybir as mybir
from concourse.bass_utils import run_bass_kernel_spmd
from contextlib import ExitStack

F32 = mybir.dt.float32
BF16 = mybir.dt.bfloat16
AF = mybir.ActivationFunctionType
ALU = mybir.AluOpType
AX = mybir.AxisListType

W = 8
B = 2
T = 8192
D = 1024
N = 16
R = 64
DFF = 2048
NE = 8
TC = 2048
L = 512
NT = TC // L
G = D // 128
FC = DFF // 128
EPS = 1e-5

MARKERS_OFF = True
PREFETCH = True
ENGS = "qgtvs"
DSEMS = ["sm", "gw", "asm", "qld0", "qld1", "qst0", "qst1",
         "xt0", "xt1", "xh", "hs", "zs", "cs", "ys", "br0", "br1",
         "st", "hf", "e1", "e2", "cr0", "cr1", "yr0", "yr1", "fin", "os", "dbg"]


class Emitter:
    def __init__(self):
        self.ops = {e: [] for e in ENGS}
        self.cnt = {e: 0 for e in ENGS}
        self.dcnt = {d: 0 for d in DSEMS}

    def op(self, e, f):
        self.ops[e].append(("op", f))

    def inc(self, e, f):
        self.cnt[e] += 1
        self.ops[e].append(("inc1", f))
        return (e, self.cnt[e])

    def dma(self, e, dsem, f):
        self.dcnt[dsem] += 16
        self.ops[e].append(("dma", (dsem, f)))
        return (dsem, self.dcnt[dsem])

    def dfull(self, dsem):
        return (dsem, self.dcnt[dsem])

    def wait(self, e, tok):
        if tok is None:
            return
        name, n = tok
        if name == e:
            return
        self.ops[e].append(("wait", (name, n)))

    def selfsync(self, e):
        self.cnt[e] += 1
        self.ops[e].append(("nop1", None))
        self.ops[e].append(("wait", (e + "!", self.cnt[e])))

    def marker(self, e):
        self.cnt[e] += 1
        self.ops[e].append(("nop1", None))
        return (e, self.cnt[e])

    def incm(self, e, f=None, site=""):
        if MARKERS_OFF is True or (isinstance(MARKERS_OFF, (set, frozenset)) and site in MARKERS_OFF):
            return self.inc(e, f)
        self.inc(e, f)
        return self.marker(e)

    def barrier(self, light=()):
        toks = {e: self.marker(e) for e in ENGS}
        for e in ENGS:
            for oe in ENGS:
                if oe != e and oe not in light:
                    self.wait(e, toks[oe])

    def run(self, block, sems):
        for key in ENGS:
            lst = self.ops[key]
            if not lst:
                continue
            body = {"q": block.sync, "g": block.gpsimd, "t": block.tensor,
                    "v": block.vector, "s": block.scalar}[key]

            def make_body(lst=lst, key=key):
                def _body(eng):
                    for kind, payload in lst:
                        if kind == "op":
                            payload(eng)
                        elif kind == "inc1":
                            payload(eng).then_inc(sems[key], 1)
                        elif kind == "dma":
                            dsem, f = payload
                            f(eng).then_inc(sems[dsem], 16)
                        elif kind == "wait":
                            name, n = payload
                            eng.wait_ge(sems[name.rstrip("!")], n)
                        elif kind == "nop1":
                            eng.nop().then_inc(sems[key], 1)
                return _body
            body(make_body())
        self.ops = {e: [] for e in ENGS}


def build_nc(dbg=False, stage="full"):
    nc = bass.Bass("TRN2", target_bir_lowering=False, debug=False,
                   detect_race_conditions=False)

    def inp(name, shape, dt=F32):
        return nc.dram_tensor(name, list(shape), dt, kind="ExternalInput").ap()

    x_h = inp("x_slice", (TC + 2, D))
    winT_h = inp("w_inT", (D, 2 * D))
    woutT_h = inp("w_outT", (D, D))
    wdtT_h = inp("w_dtT", (R, D))
    wcT_h = inp("w_cT", (N, D))
    wxa_h = inp("w_xTA", (D, R + N))
    wxb_h = inp("w_xTB", (D, N))
    rwT_h = inp("r_wT", (D, NE))
    alog_h = inp("alog_pack", (128, G * N))
    conv_h = inp("conv_pack", (128, G * 3))
    vec_h = inp("vec_pack", (128, 32))
    identb_h = inp("ident_bf", (128, 128), BF16)
    identf_h = inp("ident_f", (128, 128), F32)
    ucoef_h = inp("ucoef", (128, 8))
    qw1_h = inp("qw1", (DFF, D))
    qw2_h = inp("qw2", (D, DFF))
    qs1_h = inp("qs1", (DFF // W, D))
    qs2_h = inp("qs2", (D // W, DFF))

    out_h = nc.dram_tensor("out", [TC, D], F32, kind="ExternalOutput").ap()

    def intern(name, shape, dt=F32, dbg_vis=False):
        kind = "ExternalOutput" if (dbg and dbg_vis) else "Internal"
        return nc.dram_tensor(name, list(shape), dt, kind=kind).ap()

    h_sp = intern("h_sp", (NT * G * 128, L), BF16, dbg_vis=True)
    xn_sp = intern("xn_sp", (G * 128, L), BF16, dbg_vis=(stage == "dbgz"))
    gate_sp = intern("gate_sp", (NT * 128, 4 * NE), F32, dbg_vis=True)
    e1_sp = intern("e1_sp", (G * 128, DFF), BF16, dbg_vis=True)
    e2_sp = intern("e2_sp", (FC * 128, D), BF16, dbg_vis=True)
    s1_sp = intern("s1_sp", (G * 128, DFF), BF16, dbg_vis=True)
    wqb_sp = intern("wqb_sp", (2 * 128, DFF), BF16, dbg_vis=True)
    qst_sp = intern("qst_sp", (2 * 128, DFF), BF16, dbg_vis=True)
    s2_sp = intern("s2_sp", (FC * 128, D), BF16, dbg_vis=True)
    acc_sp = intern("acc_sp", (NT * 4 * 128, D), F32, dbg_vis=True)
    cdt_sp = intern("cdt_sp", (NT * G * 128, L), F32, dbg_vis=True)
    siluz_sp = intern("siluz_sp", (NT * 4 * 128, D), F32, dbg_vis=True)
    yp_sp = intern("yp_sp", (NT * G * 128, L), F32, dbg_vis=True)
    qT_ew1 = intern("qT_ew1", (D, DFF), BF16)
    qT_ew2 = intern("qT_ew2", (DFF, D), BF16)
    qT_s1 = intern("qT_s1", (D, DFF // W), BF16)
    qT_s2 = intern("qT_s2", (DFF, D // W), BF16)
    ag_ew1 = intern("ag_ew1", (W * D, DFF), BF16)
    ag_ew2 = intern("ag_ew2", (W * DFF, D), BF16)
    ag_s1 = intern("ag_s1", (W * D, DFF // W), BF16)
    ag_s2 = intern("ag_s2", (W * DFF, D // W), BF16)
    sw1T_lin = intern("sw1T_lin", (D, DFF), BF16)
    sw2T_lin = intern("sw2T_lin", (DFF, D), BF16)
    st_in = intern("st_in", (128, 2 * G * N))
    st_out = intern("st_out", (W * 128, 2 * G * N))

    em = Emitter()
    q, g, t, v, s = "q", "g", "t", "v", "s"

    with ExitStack() as outer:
        e_ = outer.enter_context
        nega = e_(nc.sbuf_tensor("nega", [128, G * N], F32))
        convp = e_(nc.sbuf_tensor("convp", [128, G * 3], F32))
        vecp = e_(nc.sbuf_tensor("vecp", [128, 32], F32))
        identb = e_(nc.sbuf_tensor("identb", [128, 128], BF16))
        identf = e_(nc.sbuf_tensor("identf", [128, 128], F32))
        ucoef = e_(nc.sbuf_tensor("s_ucoef", [128, 8], F32))
        ones_col = e_(nc.sbuf_tensor("ones_col", [1, 128], F32))
        constv = e_(nc.sbuf_tensor("constv", [128, 4], F32))
        nc.const_aps.register(F32, EPS, constv[:, 0:1])
        nc.const_aps.register(F32, -1.0, constv[:, 1:2])
        nc.const_aps.register(F32, 1.0, constv[:, 2:3])
        nc.const_aps.register(F32, 0.0, constv[:, 3:4])
        ones_row = e_(nc.sbuf_tensor("ones_row", [128, L], F32))
        scarry = [e_(nc.sbuf_tensor(f"scarry{j}", [128, 1], F32)) for j in range(G * N)]
        cdtcarry = [e_(nc.sbuf_tensor(f"cdtcarry{j}", [128, 1], F32)) for j in range(G)]
        scarry_cat = e_(nc.sbuf_tensor("scarry_cat", [128, G * N], F32))
        cp_bf = [e_(nc.sbuf_tensor(f"cpbf{i}", [N, L], BF16)) for i in range(NT)]
        s_in = e_(nc.sbuf_tensor("s_in", [128, G * N], F32))
        wc = e_(nc.sbuf_tensor("wc", [N, D], BF16))
        ps = [e_(nc.psum_tensor(f"ps{i}", [128, L], F32)) for i in range(6)]
        ps_t2 = [e_(nc.psum_tensor("ps_t0", [128, 128], BF16))]
        ps_tb2 = e_(nc.psum_tensor("ps_tb2", [128, 128], BF16))

        sems = {key: e_(nc.semaphore("sem_" + key)) for key in ENGS}
        for d in DSEMS:
            sems[d] = e_(nc.semaphore("dsm_" + d))
        csem = e_(nc.semaphore("sem_c"))
        ccnt = [0]
        block = e_(nc.Block())

        with ExitStack() as prep_scope:
            pe_ = prep_scope.enter_context
            qbuf = [pe_(nc.sbuf_tensor(f"qbuf{i}", [128, DFF], F32)) for i in range(2)]
            tq = pe_(nc.sbuf_tensor("tq", [128, DFF], F32))
            mng2 = [pe_(nc.sbuf_tensor(f"mng{j}", [128, DFF], F32)) for j in range(2)]
            wqb = pe_(nc.sbuf_tensor("wqb", [128, DFF], BF16))
            qstage = [pe_(nc.sbuf_tensor(f"qstage{i}", [128, DFF], BF16)) for i in range(2)]
            qcol = pe_(nc.sbuf_tensor("qcol", [128, 8], F32))
            atmp = pe_(nc.sbuf_tensor("atmp", [128, G * N], F32))

            em.dma(q, "sm", lambda e: e.dma_start(out=atmp[:, :], in_=alog_h))
            em.dma(q, "sm", lambda e: e.dma_start(out=convp[:, :], in_=conv_h))
            em.dma(q, "sm", lambda e: e.dma_start(out=vecp[:, :], in_=vec_h))
            em.dma(q, "sm", lambda e: e.dma_start(out=identb[:, :], in_=identb_h))
            em.dma(q, "sm", lambda e: e.dma_start(out=identf[:, :], in_=identf_h))
            em.dma(q, "sm", lambda e: e.dma_start(out=ucoef[:, :], in_=ucoef_h))
            em.dma(g, "gw", lambda e: e.dma_start(out=wc[:, :], in_=wcT_h))
            tok_sm = em.dfull("sm")

            em.op(v, lambda e: e.memset(ones_col[:, :], 1.0))
            em.op(v, lambda e: e.memset(constv[:, 0:1], EPS))
            em.op(v, lambda e: e.memset(constv[:, 1:2], -1.0))
            em.op(v, lambda e: e.memset(constv[:, 2:3], 1.0))
            tok_const = em.inc(v, lambda e: e.memset(constv[:, 3:4], 0.0))
            em.op(v, lambda e: e.memset(ones_row[:, :], 1.0))
            for j in range(G * N):
                em.op(v, lambda e, j=j: e.memset(scarry[j][:, :], 0.0))
            for j in range(G):
                em.op(v, lambda e, j=j: e.memset(cdtcarry[j][:, :], 0.0))
            em.inc(v, lambda e: e.memset(scarry_cat[:, :], 0.0))

            em.wait(s, tok_const)
            em.wait(g, tok_const)
            em.wait(t, tok_const)
            em.wait(s, tok_sm)
            tk_aexp = em.inc(s, lambda e: e.activation(atmp[:, :], atmp[:, :], AF.Exp))
            em.wait(v, tk_aexp)
            em.inc(v, lambda e: e.tensor_scalar(nega[:, :], atmp[:, :], -1.0, None, ALU.mult))

            qjobs = [
                (qw1_h, DFF // 128, D, qT_ew1),
                (qw2_h, D // 128, DFF, qT_ew2),
                (qs1_h, (DFF // W) // 128, D, qT_s1),
                (qs2_h, (D // W) // 128, DFF, qT_s2),
            ]
            if stage == "prep_a":
                qjobs = []
            free_qbuf = [None, None]
            free_qstage = [None, None]
            mng_free = [None, None]
            qcol_free = [None, None]
            tok_qsum_sl = [None, None]
            last_tr_evac = {}
            tok_wqb_read = [None]
            gather_pairs = [(qT_ew1, ag_ew1), (qT_ew2, ag_ew2), (qT_s1, ag_s1), (qT_s2, ag_s2)]
            pending_gathers = []

            def gather_dst(src):
                for a, b in gather_pairs:
                    if a is src:
                        return b
                raise KeyError
            chunks = []
            for (src, nrc, K, dstT) in qjobs:
                for rc in range(nrc):
                    chunks.append((src, rc, K, dstT, rc == nrc - 1))

            def q_stage1(c):
                src, rc, K, dstT, _ = chunks[c]
                sl = c % 2
                em.wait(q, free_qbuf[sl])
                tok_load = em.dma(q, "qld" + str(sl),
                                  lambda e, src=src, rc=rc, sl=sl, K=K:
                                  e.dma_start(out=qbuf[sl][:, 0:K],
                                              in_=src[rc * 128:(rc + 1) * 128, :]))
                em.wait(s, tok_load)
                em.wait(s, mng_free[sl])
                em.wait(s, qcol_free[sl])
                tok_qsum_sl[sl] = em.inc(s, lambda e, sl=sl, K=K: e.activation(
                    mng2[sl][:, 0:K], qbuf[sl][:, 0:K], AF.Abs,
                    accum_out=qcol[:, 4 * sl:4 * sl + 1]))

            def q_rest(c):
                src, rc, K, dstT, is_last = chunks[c]
                sl = c % 2
                qc = lambda j: qcol[:, 4 * sl + j:4 * sl + j + 1]
                em.wait(v, tok_qsum_sl[sl])
                tok_qs1 = em.inc(v, lambda e, K=K, qc=qc: e.tensor_scalar(
                    qc(1), qc(0), 1.0 / K, 1e-5, ALU.mult, ALU.max))
                em.wait(s, tok_qs1)
                tok_b1 = em.inc(s, lambda e, qc=qc: e.copy(qc(3), qc(1)))
                em.wait(v, tok_b1)
                tok_rec = em.inc(v, lambda e, qc=qc: e.reciprocal(qc(2), qc(1)))
                em.wait(s, tok_rec)
                tok_b2 = em.inc(s, lambda e, qc=qc: e.copy(qc(3), qc(2)))
                em.wait(v, tok_b2)
                tok_tq = em.inc(v, lambda e, sl=sl, K=K, qc=qc: e.tensor_scalar(
                    tq[:, 0:K], qbuf[sl][:, 0:K], qc(2), None, ALU.mult))
                free_qbuf[sl] = tok_tq
                em.selfsync(v)
                em.op(v, lambda e, sl=sl, K=K: e.tensor_scalar(
                    mng2[sl][:, 0:K], tq[:, 0:K], -0.5, None, ALU.is_lt))
                em.selfsync(v)
                tok_stt = em.inc(v, lambda e, sl=sl, K=K: e.scalar_tensor_tensor(
                    tq[:, 0:K], tq[:, 0:K], 0.5, mng2[sl][:, 0:K],
                    ALU.is_gt, ALU.subtract))
                mng_free[sl] = tok_stt
                em.selfsync(v)
                em.wait(v, tok_wqb_read[0])
                tok_wqbw = em.inc(v, lambda e, K=K, qc=qc: e.tensor_scalar(
                    wqb[:, 0:K], tq[:, 0:K], qc(1), None, ALU.mult))
                qcol_free[sl] = tok_wqbw
                em.selfsync(v)
                em.op(v, lambda e, sl=sl: e.tensor_copy(mng2[sl][:, 0:256], wqb[:, 0:256]))
                em.selfsync(v)
                tok_wqb = em.inc(v, lambda e, sl=sl, K=K: e.tensor_copy(
                    mng2[sl][:, 256:512], wqb[:, K - 256:K]))
                ssl = sl
                em.wait(t, tok_wqb)
                em.wait(t, tok_sm)
                em.wait(v, free_qstage[ssl])
                tok_ev = None
                for j in range(K // 128):
                    em.wait(t, last_tr_evac.get(0))
                    em.inc(t, lambda e, j=j: e.transpose(
                        ps_t2[0][:, :],
                        wqb[:, j * 128:(j + 1) * 128], identb[:, :]))
                    tok_tr = em.marker(t)
                    em.wait(v, tok_tr)
                    tok_ev = em.inc(v, lambda e, j=j, ssl=ssl: e.tensor_copy(
                        qstage[ssl][:, j * 128:(j + 1) * 128],
                        ps_t2[0][:, :]))
                    last_tr_evac[0] = tok_ev
                tok_wqb_read[0] = ("t", em.cnt["t"])
                em.wait(q, tok_ev)
                if dbg and c in (0, 3):
                    di = 0 if c == 0 else 1
                    em.dma(q, "dbg", lambda e, di=di, K=K: e.dma_start(
                        out=wqb_sp[di * 128:(di + 1) * 128, 0:K], in_=wqb[:, 0:K]))
                    em.dma(q, "dbg", lambda e, di=di, ssl=ssl, K=K: e.dma_start(
                        out=qst_sp[di * 128:(di + 1) * 128, 0:K], in_=qstage[ssl][:, 0:K]))
                    em.wait(v, em.dfull("dbg"))
                    em.wait(t, em.dfull("dbg"))
                tok_qdma = em.dma(q, "qst" + str(ssl),
                                  lambda e, ssl=ssl, rc=rc, K=K, dstT=dstT: e.dma_start(
                    out=dstT[0:K, rc * 128:(rc + 1) * 128].rearrange("(j p) c -> p j c", p=128),
                    in_=qstage[ssl][:, 0:K].rearrange("p (j c) -> p j c", c=128)))
                free_qstage[ssl] = tok_qdma
                if is_last and stage not in ("prep_a", "prep_b"):
                    ccnt[0] += 1
                    pending_gathers.append((em.dfull("qst0"), em.dfull("qst1"),
                                            dstT, gather_dst(dstT)))

            for c in range(len(chunks) + 1):
                if c < len(chunks):
                    q_stage1(c)
                if c >= 1:
                    q_rest(c - 1)

            wag_cnt = ccnt[0]

            em.barrier(light=("g",))
            if stage.startswith("prep"):
                for d in DSEMS:
                    if em.dcnt[d]:
                        em.wait(q, em.dfull(d))
                em.marker(q)
            em.run(block, dict(sems, c=csem))
        if stage.startswith("prep"):
            return nc

        with ExitStack() as p1_scope:
            p1 = p1_scope.enter_context
            wi = [p1(nc.sbuf_tensor(f"wi{k}", [128, 2 * D], BF16)) for k in range(G)]
            wdt = p1(nc.sbuf_tensor("wdt", [R, D], BF16))
            wxa = [p1(nc.sbuf_tensor(f"wxa{k}", [128, R + N], BF16)) for k in range(G)]
            wxb = [p1(nc.sbuf_tensor(f"wxb{k}", [128, N], BF16)) for k in range(G)]
            xtok = [p1(nc.sbuf_tensor(f"xtok{i}", [128, D], F32)) for i in range(2)]
            xhalo = p1(nc.sbuf_tensor("xhalo", [2, D], F32))
            lncol = p1(nc.sbuf_tensor("lncol", [128, 16], F32))
            sqscr = p1(nc.sbuf_tensor("sqscr", [128, D], F32))
            xnt_bf = p1(nc.sbuf_tensor("xnt_bf", [128, D], BF16))
            xn = [p1(nc.sbuf_tensor(f"xn{k}", [128, L + 2], BF16)) for k in range(G)]
            hfm = [p1(nc.sbuf_tensor(f"hfm{k}", [128, L], BF16)) for k in range(G)]
            xcs = [p1(nc.sbuf_tensor(f"xcs{i}", [128, L + 2], F32)) for i in range(2)]
            convt = p1(nc.sbuf_tensor("convt", [128, L], F32))
            xsf = [p1(nc.sbuf_tensor(f"xsf{k}", [128, L], F32)) for k in range(G)]
            xsb = [p1(nc.sbuf_tensor(f"xsb{k}", [128, L], BF16)) for k in range(G)]
            dtp_bf = p1(nc.sbuf_tensor("dtp_bf", [R, L], BF16))
            bp_f = p1(nc.sbuf_tensor("bp_f", [N, L], F32))
            bp_row = [p1(nc.sbuf_tensor(f"bprow{i}", [1, L], F32)) for i in range(2)]
            bpb = p1(nc.sbuf_tensor("bpb", [128, L], F32))
            dt_h = [p1(nc.sbuf_tensor(f"dth{i}", [128, L], F32)) for i in range(4)]
            cdt_s = [p1(nc.sbuf_tensor(f"cdts{i}", [128, L], F32)) for i in range(2)]
            dtxs_h = [p1(nc.sbuf_tensor(f"dtxsh{i}", [128, L], F32)) for i in range(4)]
            ssum_h = [p1(nc.sbuf_tensor(f"ssumh{i}", [128, L], F32)) for i in range(4)]
            yps = [p1(nc.sbuf_tensor(f"yps{i}", [128, L], F32)) for i in range(2)]
            sa_s = [p1(nc.sbuf_tensor(f"sas{i}", [128, L], F32)) for i in range(2)]
            sbx_s = [p1(nc.sbuf_tensor(f"sbxs{i}", [128, L], F32)) for i in range(2)]
            ss_s = [p1(nc.sbuf_tensor(f"sss{i}", [128, L], F32)) for i in range(2)]
            szslot = [p1(nc.sbuf_tensor(f"szslot{i}", [128, D], F32)) for i in range(2)]
            sp1 = p1(nc.sbuf_tensor("sp1", [128, L], F32))
            sp2 = p1(nc.sbuf_tensor("sp2", [128, L], F32))
            sp3 = p1(nc.sbuf_tensor("sp3", [128, L], F32))
            rm_sb = p1(nc.sbuf_tensor("rm_sb", [1, 2 * L], F32))

            for k in range(G):
                em.dma(g, "gw", lambda e, k=k: e.dma_start(
                    out=wi[k][:, :], in_=winT_h[k * 128:(k + 1) * 128, :]))
                em.dma(g, "gw", lambda e, k=k: e.dma_start(
                    out=wxa[k][:, :], in_=wxa_h[k * 128:(k + 1) * 128, :]))
                em.dma(g, "gw", lambda e, k=k: e.dma_start(
                    out=wxb[k][:, :], in_=wxb_h[k * 128:(k + 1) * 128, :]))
            em.dma(g, "gw", lambda e: e.dma_start(out=wdt[:, :], in_=wdtT_h))
            for (tq0, tq1, gsrc, gdst) in pending_gathers:
                em.wait(g, tq0)
                em.wait(g, tq1)
                em.op(g, lambda e, gsrc=gsrc, gdst=gdst: e.collective_compute(
                    "AllGather", ALU.bypass, replica_groups=[list(range(W))],
                    ins=[gsrc.opt()], outs=[gdst.opt()]).then_inc(csem, 1))
            tok_wmix = em.dfull("gw")
            em.wait(t, tok_wmix)
            em.wait(v, tok_wmix)
            em.wait(s, tok_wmix)
            em.wait(q, tok_wmix)
            em.wait(g, tok_wmix)

            free_xtok = [None, None]
            free_xhalo = None
            cdt_carry_done = [None] * G
            s_carry_done = [None] * (G * N)
            free_xcs = [None, None]
            free_cdt = [None, None]
            free_yps = [None, None]
            free_sa = [None, None]
            free_sbx = [None, None]
            free_ss = [None, None]
            free_ss_scpy = [None, None]
            free_bprow = [None, None]
            free_ps = [None] * 6
            free_pst = [None, None]
            rm_free = [None]
            free_bpb = [None]
            tok_xn_lastuse = [None] * G
            psum_rot = [0]
            prev_tok_dt = [None]
            last_scan_tok = None

            def next_ps01():
                idx = psum_rot[0] % 4
                psum_rot[0] += 1
                return idx

            def emit_ln(src, npart, tok_trlast):
                cc = lambda j: lncol[0:npart, j:j + 1]
                em.wait(s, ("v", em.cnt["v"]))
                em.wait(v, tok_trlast)
                em.wait(v, em.dfull("br0"))
                if stage == "dbg4":
                    em.wait(v, em.dfull("zs"))
                em.op(v, lambda e: e.tensor_reduce(cc(0), src, AX.X, ALU.add))
                tok_sq = em.inc(s, lambda e: e.activation(
                    sqscr[0:npart, :], src, AF.Square, accum_out=cc(1)))
                em.wait(v, tok_sq)
                em.selfsync(v)
                em.op(v, lambda e: e.tensor_scalar(cc(2), cc(0), 1.0 / D, None, ALU.mult))
                em.selfsync(v)
                em.op(v, lambda e: e.scalar_tensor_tensor(
                    cc(7), cc(2), -float(D), cc(0), ALU.mult, ALU.add))
                em.op(v, lambda e: e.tensor_scalar(
                    cc(3), cc(2), cc(2), float(D), ALU.mult, ALU.mult))
                em.op(v, lambda e: e.scalar_tensor_tensor(
                    cc(14), cc(2), -2.0, cc(0), ALU.mult, ALU.mult))
                em.selfsync(v)
                em.op(v, lambda e: e.tensor_tensor(cc(15), cc(3), cc(14), ALU.add))
                em.selfsync(v)
                em.op(v, lambda e: e.tensor_tensor(cc(4), cc(15), cc(1), ALU.add))
                em.selfsync(v)
                tok_var = em.inc(v, lambda e: e.tensor_scalar(
                    cc(4), cc(4), 1.0 / D, None, ALU.mult))
                em.wait(s, tok_var)
                tok_sd = em.inc(s, lambda e: e.activation(cc(5), cc(4), AF.Sqrt, bias=EPS))
                em.wait(v, tok_sd)
                em.op(v, lambda e: e.reciprocal(cc(6), cc(5)))
                em.selfsync(v)
                em.op(v, lambda e: e.tensor_scalar(cc(14), cc(7), 1.0 / D, None, ALU.mult))
                em.selfsync(v)
                em.op(v, lambda e: e.tensor_tensor(cc(8), cc(14), cc(6), ALU.mult))
                em.selfsync(v)
                em.op(v, lambda e: e.tensor_scalar(cc(14), cc(6), cc(6), None, ALU.mult))
                em.selfsync(v)
                em.op(v, lambda e: e.tensor_tensor(cc(15), cc(4), cc(14), ALU.mult))
                em.selfsync(v)
                em.op(v, lambda e: e.tensor_scalar(cc(14), cc(8), cc(8), None, ALU.mult))
                em.selfsync(v)
                tok_v2 = em.inc(v, lambda e: e.tensor_tensor(cc(9), cc(15), cc(14), ALU.subtract))
                em.wait(s, tok_v2)
                tok_sd2 = em.inc(s, lambda e: e.activation(cc(10), cc(9), AF.Sqrt, bias=EPS))
                em.wait(v, tok_sd2)
                em.op(v, lambda e: e.reciprocal(cc(11), cc(10)))
                em.selfsync(v)
                em.op(v, lambda e: e.tensor_tensor(cc(14), cc(8), cc(11), ALU.mult))
                em.selfsync(v)
                em.op(v, lambda e: e.tensor_scalar(cc(12), cc(14), -1.0, None, ALU.mult))
                em.selfsync(v)
                em.op(v, lambda e: e.tensor_scalar(
                    xnt_bf[0:npart, :], src, cc(2), cc(6), ALU.subtract, ALU.mult))
                em.selfsync(v)
                em.op(v, lambda e, npart=npart: e.tensor_copy(
                    sqscr[0:npart, 0:256], xnt_bf[0:npart, 0:256]))
                em.selfsync(v)
                tok_pre = em.inc(v, lambda e, npart=npart: e.tensor_copy(
                    sqscr[0:npart, 256:512], xnt_bf[0:npart, D - 256:D]))
                return tok_pre, tok_sq

            for i in range(NT if stage not in ("p1t1", "dbgdt", "noh") else 1):
                hs_full = em.dfull("hs")
                tok_trlast = tok_xn_lastuse[0] if i > 0 else None
                for sidx in range(5):
                    if sidx < 4:
                        sl = sidx % 2
                        if free_xtok[sl] is not None:
                            em.wait(q, free_xtok[sl][0])
                            em.wait(q, free_xtok[sl][1])
                        tok_x = em.dma(q, "xt" + str(sl),
                                       lambda e, i=i, sidx=sidx, sl=sl: e.dma_start(
                            out=xtok[sl][:, :],
                            in_=x_h[1 + i * L + sidx * 128: 1 + i * L + (sidx + 1) * 128, :]))
                        src = xtok[sl][:, :]
                        npart = 128
                    else:
                        if free_xhalo is not None:
                            em.wait(q, free_xhalo[0])
                            em.wait(q, free_xhalo[1])
                        tok_x = em.dma(q, "xh", lambda e, i=i: e.dma_start(
                            out=xhalo[:, :],
                            in_=x_h[i * L: i * L + L + 2: L + 1, :]))
                        src = xhalo[:, :]
                        npart = 2
                    em.wait(v, tok_x)
                    em.wait(s, tok_x)
                    tok_pre, tok_sq = emit_ln(src, npart, tok_trlast)
                    if sidx < 4:
                        free_xtok[sidx % 2] = (tok_pre, tok_sq)
                    else:
                        free_xhalo = (tok_pre, tok_sq)
                    for k in range(G):
                        em.wait(t, tok_pre)
                        em.wait(t, free_pst[0])
                        em.inc(t, lambda e, k=k, npart=npart: e.transpose(
                            ps_t2[0][:, 0:npart],
                            xnt_bf[0:npart, k * 128:(k + 1) * 128],
                            identb[0:npart, 0:npart]))
                        tok_tr1 = em.marker(t)
                        em.wait(v, tok_tr1)
                        em.wait(v, tok_xn_lastuse[k])
                        if sidx < 4:
                            tok_ev1 = em.inc(v, lambda e, k=k, sidx=sidx: e.tensor_copy(
                                xn[k][:, sidx * 128: (sidx + 1) * 128],
                                ps_t2[0][:, :]))
                        else:
                            tok_ev1 = em.inc(v, lambda e, k=k: e.tensor_copy(
                                xn[k][:, L:L + 2],
                                ps_t2[0][:, 0:2]))
                        free_pst[0] = tok_ev1
                    if sidx < 4 and stage != "noh":
                        em.wait(q, tok_pre)
                        em.wait(q, rm_free[0])
                        for jj in range(2):
                            em.dma(q, "br0", lambda e, sidx=sidx, jj=jj: e.dma_start(
                                out=rm_sb[0:1, jj * L + sidx * 128: jj * L + (sidx + 1) * 128]
                                .rearrange("o (a b) -> o a b", a=128),
                                in_=lncol[:, 11 + jj:12 + jj]))
                    tok_trlast = ("t", em.cnt["t"])

                if stage != "noh":
                    tok_rm = em.dfull("br0")
                    em.wait(t, tok_rm)
                    em.wait(t, free_ps[4])
                    tok_bR = em.inc(t, lambda e: e.matmul(
                        ps[4][:, :], ones_col[:, :], rm_sb[0:1, 0:L],
                        start=True, stop=True))
                    em.wait(t, free_ps[5])
                    tok_bM = em.inc(t, lambda e: e.matmul(
                        ps[5][:, :], ones_col[:, :], rm_sb[0:1, L:2 * L],
                        start=True, stop=True))
                    rm_free[0] = tok_bM
                    em.wait(v, tok_bM)
                    em.wait(v, hs_full)
                    tok_h_last = None
                    for k in range(G):
                        em.wait(v, tok_xn_lastuse[k])
                        em.op(v, lambda e, k=k: e.tensor_tensor(
                            sp1[:, :], xn[k][:, 0:L], ps[4][:, :], ALU.mult))
                        em.op(v, lambda e: e.tensor_tensor(
                            sp1[:, :], sp1[:, :], ps[5][:, :], ALU.add))
                        tok_h_last = em.inc(v, lambda e, k=k: e.tensor_scalar(
                            hfm[k][:, :], sp1[:, :],
                            vecp[:, 16 + k:17 + k], vecp[:, 24 + k:25 + k],
                            ALU.mult, ALU.add))
                        em.wait(q, tok_h_last)
                        em.dma(q, "hs", lambda e, i=i, k=k: e.dma_start(
                            out=h_sp[(i * G + k) * 128:(i * G + k + 1) * 128, :],
                            in_=hfm[k][:, :]))
                    free_ps[4] = tok_h_last
                    free_ps[5] = tok_h_last

                if stage == "dbgz" and i == 0:
                    em.wait(q, ("v", em.cnt["v"]))
                    for k in range(G):
                        em.dma(q, "os", lambda e, k=k: e.dma_start(
                            out=xn_sp[k * 128:(k + 1) * 128, :], in_=xn[k][:, 0:L]))
                for sidx in range(4):
                    szl = sidx % 2
                    em.wait(s, em.dfull("zs"))
                    tok_sz = None
                    for half in range(2):
                        pidx = next_ps01()
                        em.wait(t, free_ps[pidx])
                        npass = 1
                        for zp in range(npass):
                            for k in range(G):
                                fn = (lambda e, k=k, sidx=sidx, half=half, pidx=pidx: e.matmul(
                                    ps[pidx][:, :],
                                    xn[k][:, sidx * 128: (sidx + 1) * 128],
                                    wi[k][:, D + half * 512: D + (half + 1) * 512],
                                    start=(k == 0), stop=(k == G - 1)))
                                if k == G - 1 and zp == npass - 1:
                                    tok_z = em.incm(t, fn, 'z')
                                else:
                                    em.op(t, fn)
                        em.wait(s, tok_z)
                        tok_sz = em.inc(s, lambda e, szl=szl, half=half, pidx=pidx: e.activation(
                            szslot[szl][:, half * 512:(half + 1) * 512], ps[pidx][:, :], AF.Silu))
                        free_ps[pidx] = tok_sz
                    em.wait(q, tok_sz)
                    if stage != "dbg4":
                        em.dma(q, "zs", lambda e, i=i, sidx=sidx, szl=szl: e.dma_start(
                            out=siluz_sp[(i * 4 + sidx) * 128:(i * 4 + sidx + 1) * 128, :],
                            in_=szslot[szl][:, :]))

                em.wait(s, ("g", em.cnt["g"]))
                tok_xsb_last = None
                for gch in range(G):
                    xcl = gch % 2
                    pidx = next_ps01()
                    em.wait(t, free_ps[pidx])
                    for k in range(G):
                        fn = (lambda e, k=k, gch=gch, pidx=pidx: e.matmul(
                            ps[pidx][:, :], wi[k][:, gch * 128:(gch + 1) * 128],
                            xn[k][:, 0:L], start=(k == 0), stop=(k == G - 1)))
                        if k == G - 1:
                            tok_xc = em.incm(t, fn, 'xc')
                        else:
                            em.op(t, fn)
                    em.wait(t, free_ps[4])
                    for k in range(G):
                        fn = (lambda e, k=k, gch=gch: e.matmul(
                            ps[4][:, 0:2], wi[k][:, gch * 128:(gch + 1) * 128],
                            xn[k][:, L:L + 2], start=(k == 0), stop=(k == G - 1)))
                        if k == G - 1:
                            tok_xh2 = em.incm(t, fn, 'xh2')
                        else:
                            em.op(t, fn)
                    em.wait(v, tok_xc)
                    em.wait(v, tok_xh2)
                    em.wait(v, free_xcs[xcl])
                    em.op(v, lambda e, xcl=xcl, pidx=pidx: e.tensor_copy(
                        xcs[xcl][:, 1:L + 1], ps[pidx][:, :]))
                    tok_xcs = em.inc(v, lambda e, xcl=xcl: e.tensor_copy(
                        xcs[xcl][:, 0:(L + 2):(L + 1)], ps[4][:, 0:2]))
                    free_ps[pidx] = tok_xcs
                    free_ps[4] = tok_xcs
                    em.op(v, lambda e, xcl=xcl, gch=gch: e.tensor_scalar(
                        convt[:, :], xcs[xcl][:, 0:L], convp[:, 3 * gch:3 * gch + 1],
                        None, ALU.mult))
                    em.op(v, lambda e, xcl=xcl, gch=gch: e.scalar_tensor_tensor(
                        convt[:, :], xcs[xcl][:, 1:L + 1], convp[:, 3 * gch + 1:3 * gch + 2],
                        convt[:, :], ALU.mult, ALU.add))
                    tok_conv = em.inc(v, lambda e, xcl=xcl, gch=gch: e.scalar_tensor_tensor(
                        convt[:, :], xcs[xcl][:, 2:L + 2], convp[:, 3 * gch + 2:3 * gch + 3],
                        convt[:, :], ALU.mult, ALU.add))
                    free_xcs[xcl] = tok_conv
                    em.wait(s, tok_conv)
                    tok_xs = em.inc(s, lambda e, gch=gch: e.activation(
                        xsf[gch][:, :], convt[:, :], AF.Silu))
                    em.wait(g, tok_xs)
                    tok_xsb = em.inc(g, lambda e, gch=gch: e.tensor_copy(
                        xsb[gch][:, :], xsf[gch][:, :]))
                    tok_xsb_last = tok_xsb
                    em.wait(v, tok_xs)

                em.wait(t, tok_xsb_last)
                em.wait(t, free_ps[2])
                em.wait(t, free_ps[3])
                for k in range(G):
                    em.op(t, lambda e, k=k: e.matmul(
                        ps[2][0:R + N, :], wxa[k][:, :], xsb[k][:, :],
                        start=(k == 0), stop=(k == G - 1)))
                for k in range(G):
                    fn = (lambda e, k=k: e.matmul(
                        ps[3][0:N, :], wxb[k][:, :], xsb[k][:, :],
                        start=(k == 0), stop=(k == G - 1)))
                    if k == G - 1:
                        tok_xdbl = em.incm(t, fn, 'xdbl')
                    else:
                        em.op(t, fn)
                em.wait(v, tok_xdbl)
                em.wait(v, em.dfull("br0"))
                em.wait(v, em.dfull("br1"))
                em.op(v, lambda e: e.tensor_copy(dtp_bf[:, :], ps[2][0:R, :]))
                em.op(v, lambda e, i=i: e.tensor_copy(cp_bf[i][:, :], ps[2][R:R + N, :]))
                tok_bp = em.inc(v, lambda e: e.tensor_copy(bp_f[:, :], ps[3][0:N, :]))
                free_ps[2] = tok_bp
                free_ps[3] = tok_bp

                for half in range(2):
                    em.wait(s, ("v", em.cnt["v"]))
                    em.wait(g, ("v", em.cnt["v"]))
                    em.wait(v, ("g", em.cnt["g"]))
                    tok_dt_done = [None] * 4
                    for gg in range(4):
                        gch = half * 4 + gg
                        pidx = next_ps01()
                        em.wait(t, free_ps[pidx])
                        em.wait(t, tok_bp)
                        tok_dtm = em.incm(t, site='dtm', f=lambda e, gch=gch, pidx=pidx: e.matmul(
                            ps[pidx][:, :], wdt[:, gch * 128:(gch + 1) * 128], dtp_bf[:, :],
                            start=True, stop=True))
                        em.wait(v, tok_dtm)
                        tok_r1 = em.inc(v, lambda e, gch=gch, pidx=pidx: e.tensor_scalar(
                            sp3[:, :], ps[pidx][:, :], vecp[:, gch:gch + 1], 0.0,
                            ALU.add, ALU.max))
                        em.wait(s, tok_dtm)
                        em.wait(s, tok_r1)
                        em.wait(s, prev_tok_dt[0])
                        em.op(s, lambda e, gch=gch, pidx=pidx: e.activation(
                            sp1[:, :], ps[pidx][:, :], AF.Abs, bias=vecp[:, gch:gch + 1]))
                        em.op(s, lambda e: e.activation(sp2[:, :], sp1[:, :], AF.Exp, scale=-1.0))
                        tok_ln1p = em.inc(s, lambda e: e.activation(
                            sp1[:, :], sp2[:, :], AF.Ln, bias=1.0))
                        free_ps[pidx] = tok_ln1p
                        em.wait(v, tok_ln1p)
                        em.wait(v, tok_r1)
                        tok_dt = em.inc(v, lambda e, gg=gg: e.scalar_tensor_tensor(
                            dt_h[gg][:, :], sp1[:, :], 1e-5, sp3[:, :], ALU.add, ALU.add))
                        tok_dt_done[gg] = tok_dt
                        prev_tok_dt[0] = tok_dt
                        cl = gg % 2
                        em.wait(v, free_cdt[cl])
                        em.wait(v, cdt_carry_done[gch])
                        tok_cdtscan = em.inc(v, lambda e, gg=gg, gch=gch, cl=cl: e.tensor_tensor_scan(
                            cdt_s[cl][:, :], ones_row[:, :], dt_h[gg][:, :],
                            cdtcarry[gch][:, 0:1], ALU.mult, ALU.add))
                        em.wait(s, tok_cdtscan)
                        tok_cdt = em.inc(s, lambda e, gch=gch, cl=cl: e.copy(
                            cdtcarry[gch][:, 0:1], cdt_s[cl][:, L - 1:L]))
                        cdt_carry_done[gch] = tok_cdt
                        em.wait(q, tok_cdtscan)
                        em.wait(q, tok_cdt)
                        if stage in ("dbgdt", "dbg2", "dbg3", "dbg4"):
                            em.dma(q, "cs", lambda e, i=i, gch=gch, gg=gg: e.dma_start(
                                out=cdt_sp[(i * G + gch) * 128:(i * G + gch + 1) * 128, :],
                                in_=dt_h[gg][:, :]))
                        else:
                            em.dma(q, "cs", lambda e, i=i, gch=gch, cl=cl: e.dma_start(
                                out=cdt_sp[(i * G + gch) * 128:(i * G + gch + 1) * 128, :],
                                in_=cdt_s[cl][:, :]))
                        free_cdt[cl] = em.dfull("cs")
                        em.wait(g, tok_dt)
                        em.inc(g, lambda e, gg=gg, gch=gch: e.tensor_tensor(
                            dtxs_h[gg][:, :], dt_h[gg][:, :], xsf[gch][:, :], ALU.mult))
                    tok_dtxs_all = ("g", em.cnt["g"])

                    em.wait(q, tok_bp)
                    for n in range(N):
                        brl = n % 2
                        em.wait(q, free_bprow[brl])
                        tok_row = em.dma(q, "br" + str(brl),
                                         lambda e, n=n, brl=brl: e.dma_start(
                            out=bp_row[brl][:, :], in_=bp_f[n:n + 1, :]))
                        pb = 4 + (n % 2)
                        em.wait(t, free_ps[pb])
                        em.wait(t, tok_row)
                        tok_bc = em.incm(t, site='bc', f=lambda e, brl=brl, pb=pb: e.matmul(
                            ps[pb][:, :], ones_col[:, :], bp_row[brl][:, :],
                            start=True, stop=True))
                        free_bprow[brl] = tok_bc
                        for gg in range(4):
                            gch = half * 4 + gg
                            sl2 = (n * 4 + gg) % 2
                            em.wait(s, tok_dt_done[gg])
                            em.wait(s, free_sa[sl2])
                            tok_A = em.inc(s, lambda e, gg=gg, gch=gch, n=n, sl2=sl2: e.activation(
                                sa_s[sl2][:, :], dt_h[gg][:, :], AF.Exp,
                                scale=nega[:, gch * N + n: gch * N + n + 1]))
                            em.wait(v, tok_bc)
                            em.wait(v, tok_dtxs_all)
                            em.wait(v, free_sbx[sl2])
                            tok_Bx = em.inc(v, lambda e, gg=gg, pb=pb, sl2=sl2: e.tensor_tensor(
                                sbx_s[sl2][:, :], ps[pb][:, :], dtxs_h[gg][:, :], ALU.mult))
                            if gg == 3:
                                free_ps[pb] = tok_Bx
                            em.wait(v, tok_A)
                            em.wait(v, free_ss[sl2])
                            em.wait(v, free_ss_scpy[sl2])
                            em.wait(v, s_carry_done[gch * N + n])
                            tok_scan = em.inc(v, lambda e, sl2=sl2, gch=gch, n=n: e.tensor_tensor_scan(
                                ss_s[sl2][:, :], sa_s[sl2][:, :], sbx_s[sl2][:, :],
                                scarry[gch * N + n][:, 0:1], ALU.mult, ALU.add))
                            free_sa[sl2] = tok_scan
                            free_sbx[sl2] = tok_scan
                            em.wait(s, tok_scan)
                            tok_scpy = em.inc(s, lambda e, sl2=sl2, gch=gch, n=n: e.copy(
                                scarry[gch * N + n][:, 0:1], ss_s[sl2][:, L - 1:L]))
                            s_carry_done[gch * N + n] = tok_scpy
                            em.wait(g, tok_scan)
                            if n == 0:
                                tok_acc = em.inc(g, lambda e, gg=gg, sl2=sl2: e.tensor_copy(
                                    ssum_h[gg][:, :], ss_s[sl2][:, :]))
                            else:
                                tok_acc = em.inc(g, lambda e, gg=gg, sl2=sl2: e.tensor_tensor(
                                    ssum_h[gg][:, :], ssum_h[gg][:, :], ss_s[sl2][:, :], ALU.add))
                            free_ss[sl2] = tok_acc
                            free_ss_scpy[sl2] = tok_scpy
                            last_scan_tok = tok_acc

                    tok_pool_done = ("g", em.cnt["g"])
                    for gg in range(4):
                        gch = half * 4 + gg
                        pidx = next_ps01()
                        em.wait(t, free_ps[pidx])
                        tok_Cm = em.incm(t, site='Cm', f=lambda e, gch=gch, pidx=pidx, i=i: e.matmul(
                            ps[pidx][:, :], wc[:, gch * 128:(gch + 1) * 128], cp_bf[i][:, :],
                            start=True, stop=True))
                        em.wait(v, tok_Cm)
                        em.wait(v, tok_pool_done)
                        em.op(v, lambda e, gg=gg, pidx=pidx: e.tensor_tensor(
                            sp1[:, :], ps[pidx][:, :], ssum_h[gg][:, :], ALU.mult))
                        ypl = gg % 2
                        em.wait(v, free_yps[ypl])
                        if stage in ("dbg2", "dbg3", "dbg4"):
                            tok_yp = em.inc(v, lambda e, gch=gch, ypl=ypl: e.tensor_copy(
                                yps[ypl][:, :], xsf[gch][:, :]))
                        else:
                            tok_yp = em.inc(v, lambda e, gch=gch, ypl=ypl: e.scalar_tensor_tensor(
                                yps[ypl][:, :], xsf[gch][:, :], vecp[:, 8 + gch:9 + gch],
                                sp1[:, :], ALU.mult, ALU.add))
                        free_ps[pidx] = tok_yp
                        em.wait(q, tok_yp)
                        em.dma(q, "ys", lambda e, i=i, gch=gch, ypl=ypl: e.dma_start(
                            out=yp_sp[(i * G + gch) * 128:(i * G + gch + 1) * 128, :],
                            in_=yps[ypl][:, :]))
                        free_yps[ypl] = em.dfull("ys")

                for k in range(G):
                    tok_xn_lastuse[k] = ("t", em.cnt["t"])

            em.wait(v, ("s", em.cnt["s"]))
            tok_pt1 = None
            for gch in range(G):
                fn_pt = (lambda e, gch=gch: e.tensor_scalar(
                    sqscr[:, gch * N:(gch + 1) * N], nega[:, gch * N:(gch + 1) * N],
                    cdtcarry[gch][:, 0:1], None, ALU.mult))
                if gch == G - 1:
                    tok_pt1 = em.inc(v, fn_pt)
                else:
                    em.op(v, fn_pt)
            em.wait(s, tok_pt1)
            tok_pt = em.inc(s, lambda e: e.activation(
                sqscr[:, 0:G * N], sqscr[:, 0:G * N], AF.Exp))
            tok_cat = None
            for j in range(G * N):
                fn_cat = (lambda e, j=j: e.copy(scarry_cat[:, j:j + 1], scarry[j][:, 0:1]))
                if j == G * N - 1:
                    tok_cat = em.inc(s, fn_cat)
                else:
                    em.op(s, fn_cat)
            em.wait(q, tok_pt)
            em.wait(q, tok_cat)
            em.wait(q, last_scan_tok)
            em.dma(q, "st", lambda e: e.dma_start(out=st_in[:, 0:G * N], in_=scarry_cat[:, :]))
            em.dma(q, "st", lambda e: e.dma_start(
                out=st_in[:, G * N:2 * G * N], in_=sqscr[:, 0:G * N]))
            tok_sti = em.dfull("st")
            em.wait(g, tok_sti)
            ccnt[0] += 1
            em.op(g, lambda e: e.collective_compute(
                "AllGather", ALU.bypass, replica_groups=[list(range(W))],
                ins=[st_in.opt()], outs=[st_out.opt()]).then_inc(csem, 1))
            state_ag_cnt = ccnt[0]

            em.barrier(light=("g",))
            if stage in ("p1", "p1t1", "dbgdt", "dbg2", "dbg3", "dbg4", "noh"):
                for d in DSEMS:
                    if em.dcnt[d]:
                        em.wait(q, em.dfull(d))
                em.marker(q)
            em.run(block, dict(sems, c=csem))
        if stage in ("p1", "p1t1", "dbgdt", "dbg2", "dbg3", "dbg4", "noh"):
            return nc

        with ExitStack() as f_scope:
            fe = f_scope.enter_context
            wo = [fe(nc.sbuf_tensor(f"wo{k}", [128, D], BF16)) for k in range(G)]
            rwt = [fe(nc.sbuf_tensor(f"rwt{k}", [128, NE], BF16)) for k in range(G)]
            hf = [fe(nc.sbuf_tensor(f"hf{k}", [128, L], BF16)) for k in range(G)]
            ew1t2 = [[fe(nc.sbuf_tensor(f"ew1t{sl}_{k}", [128, DFF], BF16))
                      for k in range(G)] for sl in range(2)]
            ew2t = [fe(nc.sbuf_tensor(f"ew2t{j}", [128, D], BF16)) for j in range(FC)]
            h1s = [fe(nc.sbuf_tensor(f"h1s{j}", [128, L], BF16)) for j in range(FC)]
            acc = [fe(nc.sbuf_tensor(f"acc{si}", [128, D], F32)) for si in range(4)]
            gate = fe(nc.sbuf_tensor("gate", [128, 4 * NE], F32))
            gsc = [fe(nc.sbuf_tensor(f"gsc{j}", [128, NE], F32)) for j in range(3)]
            gcol = fe(nc.sbuf_tensor("gcol", [128, 6], F32))
            zerocol = fe(nc.sbuf_tensor("zerocol", [128, 1], F32))
            sallb = fe(nc.sbuf_tensor("sallb", [128, 2 * G * N], F32))
            stmp = [fe(nc.sbuf_tensor(f"stmp{j}", [128, G * N], F32)) for j in range(2)]
            cdtr = [fe(nc.sbuf_tensor(f"cdtr{j}", [128, L], F32)) for j in range(2)]
            pex = [fe(nc.sbuf_tensor(f"pex{j}", [128, L], F32)) for j in range(2)]
            corr = fe(nc.sbuf_tensor("corr", [128, L], F32))
            ypr = [fe(nc.sbuf_tensor(f"ypr{j}", [128, L], F32)) for j in range(2)]
            ybf = [fe(nc.sbuf_tensor(f"ybf{k}", [128, L], BF16)) for k in range(G)]
            fsp1 = fe(nc.sbuf_tensor("fsp1", [128, L], F32))
            szr1 = fe(nc.sbuf_tensor("szr1", [128, L], F32))
            xr1 = fe(nc.sbuf_tensor("xr1", [128, L], F32))
            outk = [fe(nc.sbuf_tensor(f"outk{j}", [128, D], F32)) for j in range(2)]

            for k in range(G):
                em.dma(g, "gw", lambda e, k=k: e.dma_start(
                    out=wo[k][:, :], in_=woutT_h[k * 128:(k + 1) * 128, :]))
            for k in range(G):
                em.dma(g, "gw", lambda e, k=k: e.dma_start(
                    out=rwt[k][:, :], in_=rwT_h[k * 128:(k + 1) * 128, :]))
            tok_wf = em.dfull("gw")
            em.wait(t, tok_wf)

            em.op(q, lambda e, c=state_ag_cnt: e.wait_ge(csem, c))
            em.op(v, lambda e: e.memset(s_in[:, :], 0.0))
            prev_rank_use = None
            for j in range(W):
                em.wait(q, prev_rank_use)
                tok_rj = em.dma(q, "st", lambda e, j=j: e.dma_start(
                    out=sallb[:, :], in_=st_out[j * 128:(j + 1) * 128, :]))
                em.wait(v, tok_rj)
                em.op(v, lambda e: e.tensor_tensor(
                    stmp[0][:, :], sallb[:, 128:256], s_in[:, :], ALU.mult))
                em.op(v, lambda e: e.tensor_tensor(
                    stmp[1][:, :], stmp[0][:, :], sallb[:, 0:128], ALU.add))
                em.op(v, lambda e: e.tensor_tensor(
                    stmp[0][:, :], stmp[1][:, :], s_in[:, :], ALU.subtract))
                prev_rank_use = em.inc(v, lambda e, j=j: e.scalar_tensor_tensor(
                    s_in[:, :], stmp[0][:, :], ucoef[:, j:j + 1], s_in[:, :], ALU.mult, ALU.add))
            tok_sin = em.inc(v, lambda e: e.tensor_copy(stmp[0][:, 0:1], s_in[:, 0:1]))
            em.op(v, lambda e: e.memset(zerocol[:, 0:1], 0.0))
            em.selfsync(v)

            em.op(q, lambda e, c=wag_cnt: e.wait_ge(csem, c))
            for rr in range(W):
                em.dma(q, "asm", lambda e, rr=rr: e.dma_start(
                    out=sw1T_lin[:, rr * (DFF // W):(rr + 1) * (DFF // W)],
                    in_=ag_s1[rr * D:(rr + 1) * D, :]))
                em.dma(q, "asm", lambda e, rr=rr: e.dma_start(
                    out=sw2T_lin[:, rr * (D // W):(rr + 1) * (D // W)],
                    in_=ag_s2[rr * DFF:(rr + 1) * DFF, :]))
            tok_asm = em.dfull("asm")

            free_psf = [None] * 6
            free_hf = [None] * G
            free_ew1sl = [None, None]
            tok_w1sl = [None, None]
            free_ew2 = None
            free_h1 = [None] * FC
            free_cdtr = [None, None]
            free_fin = [None, None]
            free_pex = [None, None]
            free_ypr = [None, None]
            free_outk = [None, None]
            free_acc = [None] * 4
            tok_ybf_lastuse = [None] * G

            for i in range(NT):
                for k in range(G):
                    em.wait(q, free_hf[k])
                    em.dma(q, "hf", lambda e, i=i, k=k: e.dma_start(
                        out=hf[k][:, :], in_=h_sp[(i * G + k) * 128:(i * G + k + 1) * 128, :]))
                tok_hf_full = em.dfull("hf")
                if dbg:
                    em.wait(v, em.dfull("dbg"))
                for sidx in range(4):
                    rb = sidx % 4
                    em.wait(t, tok_hf_full)
                    em.wait(t, free_psf[rb])
                    for k in range(G):
                        fn = (lambda e, k=k, sidx=sidx, rb=rb: e.matmul(
                            ps[rb][:, 0:NE], hf[k][:, sidx * 128:(sidx + 1) * 128],
                            rwt[k][:, :], start=(k == 0), stop=(k == G - 1)))
                        if k == G - 1:
                            tok_rl = em.incm(t, fn, 'rl')
                        else:
                            em.op(t, fn)
                    em.wait(v, tok_rl)
                    tok_r0 = em.inc(v, lambda e, rb=rb: e.tensor_reduce(
                        gcol[:, 0:1], ps[rb][:, 0:NE], AX.X, ALU.max))
                    em.wait(s, tok_r0)
                    tok_rb0 = em.inc(s, lambda e: e.copy(gcol[:, 4:5], gcol[:, 0:1]))
                    em.wait(v, tok_rb0)
                    tok_lg = em.inc(v, lambda e, rb=rb: e.tensor_scalar(
                        gsc[0][:, :], ps[rb][:, 0:NE], gcol[:, 0:1], None, ALU.subtract))
                    em.wait(s, tok_lg)
                    tok_ex = em.inc(s, lambda e: e.activation(gsc[1][:, :], gsc[0][:, :], AF.Exp))
                    em.wait(v, tok_ex)
                    tok_r1 = em.inc(v, lambda e: e.tensor_reduce(
                        gcol[:, 1:2], gsc[1][:, :], AX.X, ALU.add))
                    em.wait(s, tok_r1)
                    tok_rb1 = em.inc(s, lambda e: e.copy(gcol[:, 4:5], gcol[:, 1:2]))
                    em.wait(v, tok_rb1)
                    tok_r2 = em.inc(v, lambda e: e.reciprocal(gcol[:, 2:3], gcol[:, 1:2]))
                    em.wait(s, tok_r2)
                    tok_rb2 = em.inc(s, lambda e: e.copy(gcol[:, 4:5], gcol[:, 2:3]))
                    em.wait(v, tok_rb2)
                    em.op(v, lambda e: e.tensor_scalar(
                        gsc[0][:, :], gsc[1][:, :], gcol[:, 2:3], None, ALU.mult))
                    em.selfsync(v)
                    tok_r3 = em.inc(v, lambda e: e.tensor_reduce(
                        gcol[:, 3:4], gsc[0][:, :], AX.X, ALU.max))
                    em.wait(s, tok_r3)
                    tok_rb3 = em.inc(s, lambda e: e.copy(gcol[:, 4:5], gcol[:, 3:4]))
                    em.wait(v, tok_rb3)
                    em.op(v, lambda e: e.tensor_scalar(
                        gsc[1][:, :], gsc[0][:, :], gcol[:, 3:4], None, ALU.is_ge))
                    em.selfsync(v)
                    em.op(v, lambda e: e.tensor_tensor_scan(
                        gsc[2][:, :], ones_row[:, 0:NE], gsc[1][:, :],
                        zerocol[:, 0:1], ALU.mult, ALU.add))
                    em.selfsync(v)
                    em.op(v, lambda e: e.tensor_tensor(
                        gsc[2][:, :], gsc[2][:, :], gsc[1][:, :], ALU.subtract))
                    em.selfsync(v)
                    em.op(v, lambda e: e.tensor_scalar(
                        gsc[2][:, :], gsc[2][:, :], 0.0, None, ALU.is_equal))
                    em.selfsync(v)
                    em.op(v, lambda e: e.tensor_tensor(
                        gsc[1][:, :], gsc[1][:, :], gsc[2][:, :], ALU.mult))
                    em.selfsync(v)
                    tok_gate1 = em.inc(v, lambda e, sidx=sidx: e.tensor_scalar(
                        gate[:, sidx * NE:(sidx + 1) * NE], gsc[1][:, :], gcol[:, 3:4],
                        None, ALU.mult))
                    free_psf[rb] = tok_gate1
                for si in range(4):
                    em.wait(v, free_acc[si])
                    em.inc(v, lambda e, si=si: e.memset(acc[si][:, :], 0.0))
                tok_gates = ("v", em.cnt["v"])
                if dbg:
                    em.wait(q, tok_gates)
                    em.dma(q, "dbg", lambda e, i=i: e.dma_start(
                        out=gate_sp[i * 128:(i + 1) * 128, :], in_=gate[:, :]))

                def w1src(ei):
                    return ag_ew1[ei * D:(ei + 1) * D, :] if ei < NE else sw1T_lin

                def w1_load(ei):
                    sl = ei % 2
                    em.wait(q, free_ew1sl[sl])
                    for k in range(G):
                        em.dma(q, "qld" + str(sl), lambda e, k=k, ei=ei: e.dma_start(
                            out=ew1t2[ei % 2][k][:, :],
                            in_=w1src(ei)[k * 128:(k + 1) * 128, :]))
                    tok_w1sl[sl] = em.dfull("qld" + str(sl))

                if i == 0:
                    em.wait(q, tok_asm)
                if PREFETCH:
                    w1_load(0)
                for ei in range(NE + 1):
                    sl = ei % 2
                    ew1t = ew1t2[sl]
                    if ei < NE:
                        src2 = ag_ew2[ei * DFF:(ei + 1) * DFF, :]
                    else:
                        src2 = sw2T_lin
                    if PREFETCH:
                        if ei < NE:
                            w1_load(ei + 1)
                    else:
                        w1_load(ei)
                    tok_w1 = tok_w1sl[sl]
                    em.wait(q, free_ew2)
                    for fc in range(FC):
                        em.dma(q, "e2", lambda e, fc=fc, src2=src2: e.dma_start(
                            out=ew2t[fc][:, :], in_=src2[fc * 128:(fc + 1) * 128, :]))
                    tok_w2 = em.dfull("e2")
                    if dbg and i == 0 and ei in (0, NE):
                        em.wait(q, tok_w1)
                        em.wait(q, tok_w2)
                        d1, d2 = (e1_sp, e2_sp) if ei == 0 else (s1_sp, s2_sp)
                        for k in range(G):
                            em.dma(q, "dbg", lambda e, k=k, d1=d1, ew1t=ew1t: e.dma_start(
                                out=d1[k * 128:(k + 1) * 128, :], in_=ew1t[k][:, :]))
                        for fc in range(FC):
                            em.dma(q, "dbg", lambda e, fc=fc, d2=d2: e.dma_start(
                                out=d2[fc * 128:(fc + 1) * 128, :], in_=ew2t[fc][:, :]))
                        em.wait(t, em.dfull("dbg"))
                    tok_h1 = [None] * FC
                    for fc in range(FC):
                        pidx = fc % 4
                        em.wait(t, free_psf[pidx])
                        em.wait(t, tok_w1)
                        if ei == 0:
                            em.wait(t, tok_hf_full)
                        for k in range(G):
                            fn = (lambda e, k=k, fc=fc, pidx=pidx, ew1t=ew1t: e.matmul(
                                ps[pidx][:, :], ew1t[k][:, fc * 128:(fc + 1) * 128],
                                hf[k][:, :], start=(k == 0), stop=(k == G - 1)))
                            if k == G - 1:
                                tok_m1 = em.incm(t, fn, 'm1')
                            else:
                                em.op(t, fn)
                        em.wait(s, tok_m1)
                        em.wait(s, free_h1[fc])
                        tok_h1[fc] = em.inc(s, lambda e, pidx=pidx, fc=fc: e.activation(
                            h1s[fc][:, :], ps[pidx][:, :], AF.Silu))
                        free_psf[pidx] = tok_h1[fc]
                    free_ew1sl[sl] = ("t", em.cnt["t"])
                    tok_m2_half = [None, None]
                    for ht in range(2):
                        for bank in range(2, 6):
                            em.wait(t, free_psf[bank])
                        em.wait(t, tok_w2)
                        for fc in range(FC):
                            em.wait(t, tok_h1[fc])
                            for si2 in range(2):
                                si = 2 * ht + si2
                                for dh in range(2):
                                    bank = 2 + si2 * 2 + dh
                                    fn = (lambda e, fc=fc, si=si, dh=dh, bank=bank: e.matmul(
                                        ps[bank][:, :],
                                        h1s[fc][:, si * 128:(si + 1) * 128],
                                        ew2t[fc][:, dh * 512:(dh + 1) * 512],
                                        start=(fc == 0), stop=(fc == FC - 1)))
                                    if fc == FC - 1 and si2 == 1 and dh == 1:
                                        tok_m2_half[ht] = em.incm(t, fn, 'm2')
                                    else:
                                        em.op(t, fn)
                        em.wait(v, tok_m2_half[ht])
                        if ei == 0:
                            em.wait(v, tok_gates)
                        tok_ga = None
                        for si2 in range(2):
                            si = 2 * ht + si2
                            for dh in range(2):
                                bank = 2 + si2 * 2 + dh
                                if ei < NE:
                                    fn = (lambda e, si=si, dh=dh, bank=bank, ei=ei:
                                          e.scalar_tensor_tensor(
                                              acc[si][:, dh * 512:(dh + 1) * 512],
                                              ps[bank][:, :],
                                              gate[:, si * NE + ei: si * NE + ei + 1],
                                              acc[si][:, dh * 512:(dh + 1) * 512],
                                              ALU.mult, ALU.add))
                                else:
                                    fn = (lambda e, si=si, dh=dh, bank=bank: e.tensor_tensor(
                                        acc[si][:, dh * 512:(dh + 1) * 512],
                                        acc[si][:, dh * 512:(dh + 1) * 512],
                                        ps[bank][:, :], ALU.add))
                                tok_ga = em.inc(v, fn)
                        for bank in range(2, 6):
                            free_psf[bank] = tok_ga
                    free_ew2 = ("t", em.cnt["t"])
                    for fc in range(FC):
                        free_h1[fc] = ("t", em.cnt["t"])

                if dbg:
                    em.wait(q, tok_ga)
                    for si in range(4):
                        em.dma(q, "dbg", lambda e, i=i, si=si: e.dma_start(
                            out=acc_sp[(i * 4 + si) * 128:(i * 4 + si + 1) * 128, :],
                            in_=acc[si][:, :]))
                em.wait(s, tok_sin)
                em.wait(v, tok_sin)
                tok_y = None
                for gch in range(G):
                    cl = gch % 2
                    em.wait(q, free_cdtr[cl])
                    tok_cd = em.dma(q, "cr" + str(cl),
                                    lambda e, i=i, gch=gch, cl=cl: e.dma_start(
                        out=cdtr[cl][:, :],
                        in_=cdt_sp[(i * G + gch) * 128:(i * G + gch + 1) * 128, :]))
                    tok_cr = None
                    for n in range(N):
                        pxl = n % 2
                        em.wait(s, tok_cd)
                        em.wait(s, free_pex[pxl])
                        tok_P = em.inc(s, lambda e, cl=cl, gch=gch, n=n, pxl=pxl: e.activation(
                            pex[pxl][:, :], cdtr[cl][:, :], AF.Exp,
                            scale=nega[:, gch * N + n: gch * N + n + 1]))
                        em.wait(v, tok_P)
                        if n == 0:
                            tok_cr = em.inc(v, lambda e, gch=gch, n=n, pxl=pxl: e.tensor_scalar(
                                corr[:, :], pex[pxl][:, :],
                                s_in[:, gch * N + n: gch * N + n + 1], None, ALU.mult))
                        else:
                            tok_cr = em.inc(v, lambda e, gch=gch, n=n, pxl=pxl:
                                            e.scalar_tensor_tensor(
                                                corr[:, :], pex[pxl][:, :],
                                                s_in[:, gch * N + n: gch * N + n + 1],
                                                corr[:, :], ALU.mult, ALU.add))
                        free_pex[pxl] = tok_cr
                    free_cdtr[cl] = tok_cr
                    pidx = gch % 2
                    em.wait(t, free_psf[pidx])
                    tok_C2 = em.incm(t, site='C2', f=lambda e, gch=gch, i=i, pidx=pidx: e.matmul(
                        ps[pidx][:, :], wc[:, gch * 128:(gch + 1) * 128], cp_bf[i][:, :],
                        start=True, stop=True))
                    ypl = gch % 2
                    em.wait(q, free_ypr[ypl])
                    tok_yr = em.dma(q, "yr" + str(ypl),
                                    lambda e, i=i, gch=gch, ypl=ypl: e.dma_start(
                        out=ypr[ypl][:, :],
                        in_=yp_sp[(i * G + gch) * 128:(i * G + gch + 1) * 128, :]))
                    em.wait(v, tok_C2)
                    em.wait(v, tok_yr)
                    em.op(v, lambda e, pidx=pidx: e.tensor_tensor(
                        fsp1[:, :], ps[pidx][:, :], corr[:, :], ALU.mult))
                    em.wait(v, tok_ybf_lastuse[gch])
                    tok_y = em.inc(v, lambda e, gch=gch, ypl=ypl: e.tensor_tensor(
                        ybf[gch][:, :], fsp1[:, :], ypr[ypl][:, :], ALU.add))
                    free_psf[pidx] = tok_y
                    free_ypr[ypl] = tok_y

                for sidx in range(4):
                    okl = sidx % 2
                    em.wait(v, free_outk[okl])
                    tok_out = None
                    for dh in range(2):
                        em.wait(q, free_fin[0])
                        em.dma(q, "fin", lambda e, i=i, sidx=sidx, dh=dh: e.dma_start(
                            out=szr1[:, :],
                            in_=siluz_sp[(i * 4 + sidx) * 128:(i * 4 + sidx + 1) * 128,
                                         dh * 512:(dh + 1) * 512]))
                        em.dma(q, "fin", lambda e, i=i, sidx=sidx, dh=dh: e.dma_start(
                            out=xr1[:, :],
                            in_=x_h[1 + i * L + sidx * 128: 1 + i * L + (sidx + 1) * 128,
                                    dh * 512:(dh + 1) * 512]))
                        tok_fin = em.dfull("fin")
                        pidx = 2 + dh
                        em.wait(t, free_psf[pidx])
                        em.wait(t, tok_y)
                        for k in range(G):
                            fn = (lambda e, k=k, sidx=sidx, dh=dh, pidx=pidx: e.matmul(
                                ps[pidx][:, :], ybf[k][:, sidx * 128:(sidx + 1) * 128],
                                wo[k][:, dh * 512:(dh + 1) * 512],
                                start=(k == 0), stop=(k == G - 1)))
                            if k == G - 1:
                                tok_ao = em.incm(t, fn, 'ao')
                            else:
                                em.op(t, fn)
                        em.wait(v, tok_ao)
                        em.wait(v, tok_fin)
                        em.op(v, lambda e, okl=okl, dh=dh, pidx=pidx: e.tensor_tensor(
                            outk[okl][:, dh * 512:(dh + 1) * 512], ps[pidx][:, :],
                            szr1[:, :], ALU.mult))
                        em.op(v, lambda e, okl=okl, dh=dh: e.tensor_tensor(
                            outk[okl][:, dh * 512:(dh + 1) * 512],
                            outk[okl][:, dh * 512:(dh + 1) * 512],
                            xr1[:, :], ALU.add))
                        tok_out = em.inc(v, lambda e, okl=okl, dh=dh, sidx=sidx: e.tensor_tensor(
                            outk[okl][:, dh * 512:(dh + 1) * 512],
                            outk[okl][:, dh * 512:(dh + 1) * 512],
                            acc[sidx][:, dh * 512:(dh + 1) * 512], ALU.add))
                        free_psf[pidx] = tok_out
                        free_fin[0] = tok_out
                    free_acc[sidx] = tok_out
                    em.wait(q, tok_out)
                    em.dma(q, "os", lambda e, i=i, sidx=sidx, okl=okl: e.dma_start(
                        out=out_h[i * L + sidx * 128: i * L + (sidx + 1) * 128, :],
                        in_=outk[okl][:, :]))
                    free_outk[okl] = em.dfull("os")
                for k in range(G):
                    free_hf[k] = ("t", em.cnt["t"])
                    tok_ybf_lastuse[k] = ("t", em.cnt["t"])

            em.wait(q, em.dfull("os"))
            em.marker(q)
            em.run(block, dict(sems, c=csem))

    return nc


def host_inputs(inputs):
    x = np.asarray(inputs["L0_attn_x"], dtype=np.float32)
    W_in = np.asarray(inputs["L0_attn_W_in"], dtype=np.float32)
    conv_w = np.asarray(inputs["L0_attn_conv_w"], dtype=np.float32)
    W_x = np.asarray(inputs["L0_attn_W_x"], dtype=np.float32)
    W_dt = np.asarray(inputs["L0_attn_W_dt"], dtype=np.float32)
    b_dt = np.asarray(inputs["L0_attn_b_dt"], dtype=np.float32)
    A_log = np.asarray(inputs["L0_attn_A_log"], dtype=np.float32)
    D_param = np.asarray(inputs["L0_attn_D_param"], dtype=np.float32)
    W_C = np.asarray(inputs["L0_attn_W_C"], dtype=np.float32)
    W_out = np.asarray(inputs["L0_attn_W_out"], dtype=np.float32)
    sw1 = np.asarray(inputs["L0_ffn_shared_w1"], dtype=np.float32)
    sw2 = np.asarray(inputs["L0_ffn_shared_w2"], dtype=np.float32)
    ew1 = np.asarray(inputs["L0_ffn_expert_w1"], dtype=np.float32)
    ew2 = np.asarray(inputs["L0_ffn_expert_w2"], dtype=np.float32)
    rw = np.asarray(inputs["L0_ffn_router_w"], dtype=np.float32)
    gam = np.asarray(inputs["L0_ffn_ln_gamma"], dtype=np.float32)
    bet = np.asarray(inputs["L0_ffn_ln_beta"], dtype=np.float32)

    xpad = np.zeros((B, T + 2, D), dtype=np.float32)
    xpad[:, 1:-1, :] = x

    winT = np.ascontiguousarray(W_in.T)
    woutT = np.ascontiguousarray(W_out.T)
    wdtT = np.ascontiguousarray(W_dt.T)
    wcT = np.ascontiguousarray(W_C.T)
    wxa = np.ascontiguousarray(np.concatenate([W_x[0:R], W_x[R + N:R + 2 * N]], axis=0).T)
    wxb = np.ascontiguousarray(W_x[R:R + N].T)
    rwT = np.ascontiguousarray(rw.T)
    alog_pack = np.ascontiguousarray(
        A_log.reshape(G, 128, N).transpose(1, 0, 2).reshape(128, G * N))
    conv_pack = np.ascontiguousarray(
        conv_w.reshape(G, 128, 3).transpose(1, 0, 2).reshape(128, G * 3))
    vec_pack = np.ascontiguousarray(np.concatenate([
        b_dt.reshape(G, 128).T, D_param.reshape(G, 128).T,
        gam.reshape(G, 128).T, bet.reshape(G, 128).T], axis=1))
    identb = np.eye(128, dtype=ml_dtypes.bfloat16)

    in_maps = []
    for c in range(W):
        b, ch = c // 4, c % 4
        ucoef = np.zeros((128, 8), dtype=np.float32)
        for j in range(W):
            if j // 4 == b and j % 4 < ch:
                ucoef[:, j] = 1.0
        in_maps.append({
            "x_slice": np.ascontiguousarray(xpad[b, ch * TC: (ch + 1) * TC + 2, :]),
            "w_inT": winT, "w_outT": woutT, "w_dtT": wdtT, "w_cT": wcT,
            "w_xTA": wxa, "w_xTB": wxb, "r_wT": rwT,
            "alog_pack": alog_pack, "conv_pack": conv_pack, "vec_pack": vec_pack,
            "ident_bf": identb, "ident_f": np.eye(128, dtype=np.float32), "ucoef": ucoef,
            "qw1": np.ascontiguousarray(ew1[c]),
            "qw2": np.ascontiguousarray(ew2[c]),
            "qs1": np.ascontiguousarray(sw1[c * (DFF // W):(c + 1) * (DFF // W)]),
            "qs2": np.ascontiguousarray(sw2[c * (D // W):(c + 1) * (D // W)]),
        })
    return in_maps


def kernel(**inputs):
    in_maps = host_inputs(inputs)
    nc = build_nc(dbg=False)
    res = run_bass_kernel_spmd(nc, in_maps, core_ids=list(range(W)))
    out = np.empty((B, T, D), dtype=np.float32)
    for c in range(W):
        b, ch = c // 4, c % 4
        out[b, ch * TC:(ch + 1) * TC, :] = np.asarray(res.results[c]["out"])
    return out
